# version 08
# baseline (speedup 1.0000x reference)
import contextlib

import numpy as np
import ml_dtypes
import concourse.bass as bass
import concourse.mybir as mybir
from concourse.bass_utils import run_bass_kernel_spmd

from concourse.compiler_utils import get_compiler_flags, set_compiler_flags

_flags = get_compiler_flags()
_patched = [f.replace("--enable-ldw-opt=false", "--enable-ldw-opt=true")
            for f in _flags]
if _patched != _flags:
    set_compiler_flags(_patched)

F32 = mybir.dt.float32
BF16 = mybir.dt.bfloat16
I32 = mybir.dt.int32
AF = mybir.ActivationFunctionType
ALU = mybir.AluOpType
AX = mybir.AxisListType
BFD = ml_dtypes.bfloat16

S, D, H, DK = 8192, 4096, 32, 128
NCORES = 8
TOWN = S // NCORES
NCH = D // 128
NEXP = 15
EPS = 1e-5
CSCL = float(1.0 / np.sqrt(DK))
M0 = 8.0
PASS_T = 1024
OB = 128
NOB = D // OB
NBLK = NOB * (TOWN // 128)

ENG_NAMES = ("sync", "gpsimd", "vector", "scalar", "tensor")


TRUNC = [None, 0]


class Prog:
    def __init__(self, sems, fence=()):
        self.sems = sems
        self.fence = [(s, v) for (s, v) in fence if v > 0]
        self.steps = []

    def st(self, eng, emit, deps=(), inc=None, inck=1, siminc=None):
        if TRUNC[0] is not None:
            if TRUNC[1] >= TRUNC[0]:
                return
            TRUNC[1] += 1
        self.steps.append((eng, [(s, v) for (s, v) in deps if v > 0], emit, inc,
                           inck, siminc))


ALL_STEPS = []


def flush(block, prog):
    if prog.fence:
        seen = set()
        for i, st_ in enumerate(prog.steps):
            if st_[0] not in seen:
                seen.add(st_[0])
                prog.steps[i] = (st_[0], list(prog.fence) + st_[1], *st_[2:])
    ALL_STEPS.extend(prog.steps)
    for name in ENG_NAMES:
        steps = [s for s in prog.steps if s[0] == name]
        if not steps:
            continue

        def run(E, steps=steps, sems=prog.sems):
            waited = {}
            for (_e, deps, emit, inc, inck, _si) in steps:
                for (s, v) in deps:
                    if waited.get(s, 0) < v:
                        E.wait_ge(sems[s], v)
                        waited[s] = v
                if emit is not None:
                    ins = emit(E)
                    if inc is not None:
                        ins.then_inc(sems[inc], inck)
        getattr(block, name)(run)


def build_nc(pair=False, debug=False, phases="abcd", kv_t=None):
    KV_T = kv_t if kv_t else (S // 2 if pair else S)
    NT_KV = KV_T // 128
    NPASS = KV_T // PASS_T
    TBK = PASS_T // 512
    NEB = D // 256
    NTT = PASS_T // 128
    NGK = NPASS * NCH * TBK
    NGV = NPASS * NEB * NTT

    nc = bass.Bass("TRN2", target_bir_lowering=False)

    xf = nc.dram_tensor("xf", [S, D], F32, kind="ExternalInput").ap()
    xo = nc.dram_tensor("xo", [TOWN, D], F32, kind="ExternalInput").ap()
    wqT = nc.dram_tensor("wqT", [D, D], F32, kind="ExternalInput").ap()
    wkT = nc.dram_tensor("wkT", [D, D], F32, kind="ExternalInput").ap()
    wvT = nc.dram_tensor("wvT", [D, D], F32, kind="ExternalInput").ap()
    woT = nc.dram_tensor("woT", [D, D], F32, kind="ExternalInput").ap()
    bWT = nc.dram_tensor("bWT", [D, D], F32, kind="ExternalInput").ap()
    rWT = nc.dram_tensor("rWT", [D, NEXP], F32, kind="ExternalInput").ap()
    AaTd = nc.dram_tensor("AaT", [D, 128], F32, kind="ExternalInput").ap()
    BaTd = nc.dram_tensor("BaT", [128, D], F32, kind="ExternalInput").ap()
    bbd = nc.dram_tensor("bb", [D], F32, kind="ExternalInput").ap()
    rbd = nc.dram_tensor("rb", [NEXP], F32, kind="ExternalInput").ap()
    emd = nc.dram_tensor("em", [NEXP], I32, kind="ExternalInput").ap()
    identbf = nc.dram_tensor("identbf", [128, 128], BF16, kind="ExternalInput").ap()
    metad = nc.dram_tensor("meta", [4], I32, kind="ExternalInput").ap()
    out = nc.dram_tensor("out", [TOWN, D], F32, kind="ExternalOutput").ap()

    sp = dict(addr_space="Shared") if pair else {}
    xnT_d = nc.dram_tensor("xnT_d", [NCH, 128, KV_T], BF16, kind="Internal").ap()
    kvK = nc.dram_tensor("kvK", [D, S], BF16, kind="Internal", **sp).ap()
    kvV = nc.dram_tensor("kvV", [S, D], BF16, kind="Internal", **sp).ap()
    xnoT_d = nc.dram_tensor("xnoT_d", [128, NCH * TOWN], BF16, kind="Internal").ap()
    shfl = (nc.dram_tensor("shfl", [2, 128], I32, kind="Internal",
                           addr_space="Shared").ap() if pair else None)

    dbg = {}
    if debug:
        for nm, shp, dt in [
            ("dbg_q", [128, H * TOWN], BF16),
            ("dbg_attT", [128, NCH * TOWN], BF16),
            ("dbg_hw", [128, TOWN], BF16),
            ("dbg_wexp", [128, TOWN], F32),
            ("dbg_xnoT", [128, NCH * TOWN], BF16),
            ("dbg_k", [384, S], BF16),
            ("dbg_v", [512, D], BF16),
        ]:
            dbg[nm] = nc.dram_tensor(nm, shp, dt, kind="ExternalOutput").ap()

    sem_names = [
        "sA_x", "sA_v", "sA_a", "sA_tp", "sA_ev", "sA_st",
        "sK_w", "sK_g", "sK_ev", "sK_st", "sP_x",
        "sV_w", "sV_g", "sV_ev", "sV_st",
        "sB_x", "sB_v", "sB_a", "sB_p", "sB_tp", "sB_ev", "sB_g",
        "sQ_w", "sQ_g", "sQ_e",
        "sC_kv", "sC_s", "sC_e", "sC_pv", "sC_rl", "sC_am", "sC_tr", "sC_ae",
        "sD_w", "sD_x", "sD_m", "sD_a", "sD_o", "sM_c", "sM_g",
        "sPar", "sFl", "sPoll",
    ]

    with contextlib.ExitStack() as ctx:
        sems = {n: ctx.enter_context(nc.semaphore(n)) for n in sem_names}
        sb_ident = ctx.enter_context(nc.sbuf_tensor("sb_ident", [128, 128], BF16))
        sb_identf = ctx.enter_context(nc.sbuf_tensor("sb_identf", [128, 128], F32))
        sb_ones = ctx.enter_context(nc.sbuf_tensor("sb_ones", [1, 128], BF16))
        sb_hw = ctx.enter_context(nc.sbuf_tensor("sb_hw", [128, TOWN], BF16))
        sb_cst = ctx.enter_context(nc.sbuf_tensor("sb_cst", [128, 4], F32))
        sb_par = ctx.enter_context(nc.sbuf_tensor("sb_par", [1, 2], I32))
        sb_meta = ctx.enter_context(nc.sbuf_tensor("sb_meta2", [1, 4], I32))
        sb_fl = ctx.enter_context(nc.sbuf_tensor("sb_fl", [1, 1], I32))
        pS = ctx.enter_context(nc.psum_tensor("pS", [128, 1024], F32))
        po = ctx.enter_context(nc.psum_tensor("po", [128, 1536], F32))
        pt = ctx.enter_context(nc.psum_tensor("pt", [128, 2048], BF16))
        psE = ctx.enter_context(nc.psum_tensor("psE", [128, 512], F32))
        PO_OFF = [0, 129, 258, 512, 641, 770, 1024, 1153]
        PO_START = {0, 3, 6}
        block = ctx.enter_context(nc.Block())

        par_regs = {}

        def get_preg(E, col=0):
            k = (str(E.engine), col)
            if k not in par_regs:
                par_regs[k] = E.alloc_register(f"parreg_{k[0]}_{col}")
                v = E.value_load(sb_par[0:1, col:col + 1])
                E.reg_mov(par_regs[k], v)
            return par_regs[k]

        def par_dma(E, emit0, emit1, inc_h=None, inck=16):
            pr = get_preg(E, 0)
            with E.If_eq(pr, 0):
                i0 = emit0(E)
                if inc_h is not None:
                    i0.then_inc(sems[inc_h], inck)
            with E.Else():
                i1 = emit1(E)
                if inc_h is not None:
                    i1.then_inc(sems[inc_h], inck)

        if pair:
            @block.gpsimd
            def _(g):
                for arm in g.switch_core_id(n=128):
                    parity = int(arm) & 1
                    g.memset(sb_par[0:1, 0:1], parity)
                    g.memset(sb_par[0:1, 1:2], 1 - parity).then_inc(sems["sPar"], 1)
        p0 = Prog(sems)
        if pair:
            p0.st("sync", lambda E: E.dma_start(
                    out=sb_meta[:, :], in_=metad.rearrange("(o n) -> o n", o=1)),
                  inc="sFl", inck=16)
        p0.st("sync", lambda E: E.dma_start(out=sb_ident[:, :], in_=identbf[:, :]),
              inc="sM_c", inck=16)
        p0.st("vector", lambda E: E.memset(sb_ones[:, :], 1.0), inc="sB_v")
        p0.st("vector", lambda E: E.tensor_copy(out=sb_identf[:, :], in_=sb_ident[:, :]),
              deps=[("sM_c", 16)], inc="sB_v")
        for ci, cv in enumerate([1.0 / D, EPS, CSCL, -M0]):
            p0.st("vector", lambda E, ci=ci, cv=cv: E.memset(sb_cst[:, ci:ci + 1], cv),
                  inc="sB_v")
        flush(block, p0)
        CSTV = 6
        cnt = {"sB_v": 6, "sB_a": 0, "sB_p": 0, "sB_g": 0, "sM_c": 16, "sM_g": 0,
               "sB_x": 0, "sB_tp": 0, "sB_ev": 0}

        DO_A = "a" in phases
        DO_B = "b" in phases
        DO_C = "c" in phases
        DO_D = "d" in phases
        if DO_A:
            with (
                nc.sbuf_tensor("sb_ax0", [128, D], F32) as ax0,
                nc.sbuf_tensor("sb_ax1", [128, D], F32) as ax1,
                nc.sbuf_tensor("sb_asq0", [128, D], BF16) as asq0,
                nc.sbuf_tensor("sb_asq1", [128, D], BF16) as asq1,
                nc.sbuf_tensor("sb_axn0", [128, D], BF16) as axn0,
                nc.sbuf_tensor("sb_axn1", [128, D], BF16) as axn1,
                nc.sbuf_tensor("sb_ast0", [128, D], BF16) as ast0,
                nc.sbuf_tensor("sb_ast1", [128, D], BF16) as ast1,
                nc.sbuf_tensor("sb_stat", [128, 16], F32) as sb_stat,
            ):
                axs, asqs, axns, asts = [ax0, ax1], [asq0, asq1], [axn0, axn1], [ast0, ast1]
                pA = Prog(sems)
                M1, S2, Uc, SD, RS, MP, Qc = 0, 1, 2, 3, 4, 5, 6

                def a0_ln(p, i, xin_rows, src, bufs, sems_pfx, extra_x_dep=()):
                    sx, sv, sa, stp = sems_pfx
                    b = i % 2
                    ax, asq, axn = bufs[0][b], bufs[1][b], bufs[2][b]
                    st_ = lambda k, b=b: sb_stat[:, b * 8 + k: b * 8 + k + 1]
                    if pair:
                        p.st("sync", lambda E, ax=ax: par_dma(
                                E,
                                lambda E2: E2.dma_start(
                                    out=ax[:, :],
                                    in_=src[xin_rows[0]:xin_rows[1], :]),
                                lambda E2: E2.dma_start(
                                    out=ax[:, :],
                                    in_=src[KV_T + xin_rows[0]:
                                            KV_T + xin_rows[1], :]),
                                inc_h=sx),
                             deps=[(sv, 6 * (i - 1)), ("sPar", 1)]
                                  + list(extra_x_dep), siminc=(sx, 16))
                    else:
                        p.st("sync", lambda E, ax=ax: E.dma_start(
                                out=ax[:, :], in_=src[xin_rows[0]:xin_rows[1], :]),
                             deps=[(sv, 6 * (i - 1))] + list(extra_x_dep),
                             inc=sx, inck=16)
                    p.st("vector", lambda E, ax=ax, st_=st_: E.reduce_sum(
                            st_(M1), ax[:, :], axis=AX.X),
                         deps=[(sx, 16 * (i + 1)), (sa, 3 * (i - 2) + 2)], inc=sv)
                    p.st("scalar", lambda E, ax=ax, asq=asq: E.activation(
                            out=asq[:, :], in_=ax[:, :], func=AF.Square),
                         deps=[(sx, 16 * (i + 1)), (sv, 6 * (i - 2) + 2)], inc=sa)
                    p.st("scalar", lambda E, st_=st_: E.activation(
                            out=st_(Qc), in_=st_(M1), func=AF.Square),
                         deps=[(sv, 6 * i + 1)], inc=sa)
                    p.st("vector", lambda E, asq=asq, st_=st_: E.reduce_sum(
                            st_(S2), asq[:, :], axis=AX.X),
                         deps=[(sa, 3 * i + 1)], inc=sv)
                    p.st("vector", lambda E, st_=st_: E.scalar_tensor_tensor(
                            out=st_(Uc), in0=st_(Qc), scalar=-1.0 / D, in1=st_(S2),
                            op0=ALU.mult, op1=ALU.add),
                         deps=[(sv, 6 * i + 2), (sa, 3 * i + 2)], inc=sv)
                    p.st("scalar", lambda E, st_=st_: E.activation(
                            out=st_(SD), in_=st_(Uc), func=AF.Sqrt,
                            bias=sb_cst[:, 1:2], scale=sb_cst[:, 0:1]),
                         deps=[(sv, 6 * i + 3), ("sB_v", CSTV)], inc=sa)
                    p.st("vector", lambda E, st_=st_: E.reciprocal(st_(RS), st_(SD)),
                         deps=[(sa, 3 * i + 3)], inc=sv)
                    p.st("vector", lambda E, st_=st_: E.scalar_tensor_tensor(
                            out=st_(MP), in0=st_(M1), scalar=1.0 / D, in1=st_(RS),
                            op0=ALU.mult, op1=ALU.mult),
                         deps=[(sv, 6 * i + 4)], inc=sv)
                    p.st("vector", lambda E, ax=ax, axn=axn, st_=st_: E.tensor_scalar(
                            out=axn[:, :], in0=ax[:, :], scalar1=st_(RS), scalar2=st_(MP),
                            op0=ALU.mult, op1=ALU.subtract),
                         deps=[(sv, 6 * i + 5), (stp, 16 * (i - 1))], inc=sv)

                xnTr = xnT_d.rearrange("c p t -> p c t")
                for i in range(NT_KV):
                    b = i % 2
                    a0_ln(pA, i, (i * 128, i * 128 + 128), xf, (axs, asqs, axns),
                          ("sA_x", "sA_v", "sA_a", "sA_tp"))
                    for g in range(16):
                        G = 16 * i + g
                        half = g % 2

                        def em_tp(E, b=b, g=g, half=half):
                            ins = None
                            for k in range(2):
                                c = 2 * g + k
                                ins = E.transpose(
                                    pt[:, half * 1024 + k * 128: half * 1024 + (k + 1) * 128],
                                    axns[b][:, c * 128:(c + 1) * 128], sb_ident[:, :])
                            return ins
                        pA.st("tensor", em_tp,
                              deps=[("sA_v", 6 * i + 6), ("sA_ev", G - 1)], inc="sA_tp")

                        def em_ev(E, b=b, g=g, half=half):
                            return E.copy(out=asts[b][:, (2 * g) * 128:(2 * g + 2) * 128],
                                          in_=pt[:, half * 1024: half * 1024 + 256])
                        pA.st("scalar", em_ev,
                              deps=[("sA_tp", G + 1), ("sA_st", 16 * (i - 1))], inc="sA_ev")

                    def em_st(E, b=b, i=i):
                        return E.dma_start(
                            out=xnTr[:, :, i * 128:(i + 1) * 128],
                            in_=asts[b].ap().rearrange("p (c t) -> p c t", t=128))
                    pA.st("sync", em_st, deps=[("sA_ev", 16 * (i + 1))], inc="sA_st", inck=16)
                flush(block, pA)

            with (
                nc.sbuf_tensor("sb_xnp", [128, NCH * PASS_T], BF16) as sb_xnp,
                nc.sbuf_tensor("sb_kw0", [128, NCH * 128], BF16) as kw0,
                nc.sbuf_tensor("sb_kw1", [128, NCH * 128], BF16) as kw1,
                nc.sbuf_tensor("sb_vw0", [128, NCH * 256], BF16) as vw0,
                nc.sbuf_tensor("sb_vw1", [128, NCH * 256], BF16) as vw1,
                nc.sbuf_tensor("sb_kst0", [128, 512], BF16) as kst0,
                nc.sbuf_tensor("sb_kst1", [128, 512], BF16) as kst1,
                nc.sbuf_tensor("sb_vst0", [128, 256], BF16) as vst0,
                nc.sbuf_tensor("sb_vst1", [128, 256], BF16) as vst1,
            ):
                kws, vws, ksts, vsts = [kw0, kw1], [vw0, vw1], [kst0, kst1], [vst0, vst1]
                wkTr = wkT.rearrange("(c p) e -> p c e", p=128)
                wvTr = wvT.rearrange("(c p) e -> p c e", p=128)
                xnp_r = sb_xnp.ap().rearrange("p (c t) -> p c t", t=PASS_T)
                FENCE_A0 = [("sA_st", 16 * NT_KV), ("sA_tp", 16 * NT_KV),
                            ("sA_ev", 16 * NT_KV), ("sA_v", 6 * NT_KV),
                            ("sA_a", 3 * NT_KV), ("sA_x", 16 * NT_KV)]
                pB12 = Prog(sems, fence=FENCE_A0)
                gk = 0
                gv = 0
                for p_ in range(NPASS):
                    def em_pl(E, p_=p_):
                        return E.dma_start(
                            out=xnp_r,
                            in_=xnTr[:, :, p_ * PASS_T:(p_ + 1) * PASS_T])
                    pB12.st("sync", em_pl,
                            deps=[("sA_st", 16 * (p_ + 1) * (PASS_T // 128)),
                                  ("sK_g", gk), ("sV_g", gv)],
                            inc="sP_x", inck=16)
                    for e in range(NCH):
                        L = p_ * NCH + e

                        def em_kw(E, L=L, e=e):
                            return E.dma_start(
                                out=kws[L % 2].ap().rearrange("p (c e) -> p c e", e=128),
                                in_=wkTr[:, :, e * 128:(e + 1) * 128])
                        pB12.st("gpsimd", em_kw, deps=[("sK_g", TBK * (L - 1))],
                                inc="sK_w", inck=16)
                        for tb in range(TBK):
                            def em_kg(E, L=L, e=e, tb=tb, p_=p_, gk=gk):
                                reg = pS[:, (gk % 2) * 512:(gk % 2) * 512 + 512]
                                ins = None
                                for d_ in range(NCH):
                                    ins = E.matmul(
                                        reg,
                                        kws[L % 2][:, d_ * 128:(d_ + 1) * 128],
                                        sb_xnp[:, d_ * PASS_T + tb * 512:
                                               d_ * PASS_T + tb * 512 + 512],
                                        start=(d_ == 0), stop=(d_ == NCH - 1))
                                return ins
                            pB12.st("tensor", em_kg,
                                    deps=[("sK_w", 16 * (L + 1)), ("sP_x", 16 * (p_ + 1)),
                                          ("sK_ev", gk - 1)],
                                    inc="sK_g")

                            def em_kev(E, gk=gk):
                                return E.copy(out=ksts[gk % 2][:, :],
                                              in_=pS[:, (gk % 2) * 512:(gk % 2) * 512 + 512])
                            pB12.st("scalar", em_kev,
                                    deps=[("sK_g", gk + 1), ("sK_st", 16 * (gk - 1))],
                                    inc="sK_ev")

                            colb = p_ * PASS_T + tb * 512

                            if pair:
                                def em_kst(E, gk=gk, e=e, colb=colb):
                                    par_dma(
                                        E,
                                        lambda E2: E2.dma_start(
                                            out=kvK[e * 128:(e + 1) * 128,
                                                    colb:colb + 512],
                                            in_=ksts[gk % 2][:, :]),
                                        lambda E2: E2.dma_start(
                                            out=kvK[e * 128:(e + 1) * 128,
                                                    KV_T + colb:KV_T + colb + 512],
                                            in_=ksts[gk % 2][:, :]),
                                        inc_h="sK_st")
                                pB12.st("scalar", em_kst,
                                        deps=[("sK_ev", gk + 1), ("sPar", 1)],
                                        siminc=("sK_st", 16))
                            else:
                                def em_kst(E, gk=gk, e=e, colb=colb):
                                    return E.dma_start(
                                        out=kvK[e * 128:(e + 1) * 128,
                                                colb:colb + 512],
                                        in_=ksts[gk % 2][:, :])
                                pB12.st("scalar", em_kst, deps=[("sK_ev", gk + 1)],
                                        inc="sK_st", inck=16)
                            gk += 1
                    for eb in range(NEB):
                        L2 = p_ * NEB + eb

                        def em_vw(E, L2=L2, eb=eb):
                            return E.dma_start(
                                out=vws[L2 % 2].ap().rearrange("p (c e) -> p c e", e=256),
                                in_=wvTr[:, :, eb * 256:(eb + 1) * 256])
                        pB12.st("gpsimd", em_vw, deps=[("sV_g", NTT * (L2 - 1))],
                                inc="sV_w", inck=16)
                        for tt in range(NTT):
                            def em_vg(E, L2=L2, tt=tt, p_=p_, gv=gv):
                                reg = po[:, (gv % 2) * 512:(gv % 2) * 512 + 256]
                                ins = None
                                for d_ in range(NCH):
                                    ins = E.matmul(
                                        reg,
                                        sb_xnp[:, d_ * PASS_T + tt * 128:
                                               d_ * PASS_T + tt * 128 + 128],
                                        vws[L2 % 2][:, d_ * 256:(d_ + 1) * 256],
                                        start=(d_ == 0), stop=(d_ == NCH - 1))
                                return ins
                            pB12.st("tensor", em_vg,
                                    deps=[("sV_w", 16 * (L2 + 1)), ("sP_x", 16 * (p_ + 1)),
                                          ("sV_ev", gv - 1)],
                                    inc="sV_g")

                            def em_vev(E, gv=gv):
                                return E.tensor_copy(
                                    out=vsts[gv % 2][:, :],
                                    in_=po[:, (gv % 2) * 512:(gv % 2) * 512 + 256])
                            pB12.st("vector", em_vev,
                                    deps=[("sV_g", gv + 1), ("sV_st", 16 * (gv - 1))],
                                    inc="sV_ev")

                            rowb = p_ * PASS_T + tt * 128

                            if pair:
                                def em_vst(E, gv=gv, eb=eb, rowb=rowb):
                                    par_dma(
                                        E,
                                        lambda E2: E2.dma_start(
                                            out=kvV[rowb:rowb + 128,
                                                    eb * 256:(eb + 1) * 256],
                                            in_=vsts[gv % 2][:, :]),
                                        lambda E2: E2.dma_start(
                                            out=kvV[KV_T + rowb:KV_T + rowb + 128,
                                                    eb * 256:(eb + 1) * 256],
                                            in_=vsts[gv % 2][:, :]),
                                        inc_h="sV_st")
                                pB12.st("scalar", em_vst,
                                        deps=[("sV_ev", gv + 1), ("sPar", 1)],
                                        siminc=("sV_st", 16))
                            else:
                                def em_vst(E, gv=gv, eb=eb, rowb=rowb):
                                    return E.dma_start(
                                        out=kvV[rowb:rowb + 128,
                                                eb * 256:(eb + 1) * 256],
                                        in_=vsts[gv % 2][:, :])
                                pB12.st("scalar", em_vst, deps=[("sV_ev", gv + 1)],
                                        inc="sV_st", inck=16)
                            gv += 1
                if pair:
                    def em_flag(E):
                        par_dma(
                            E,
                            lambda E2: E2.dma_start(out=shfl[0, 0:1],
                                                    in_=sb_meta[0:1, 0:1]),
                            lambda E2: E2.dma_start(out=shfl[1, 0:1],
                                                    in_=sb_meta[0:1, 0:1]),
                            inc_h="sFl")
                    pB12.st("gpsimd", em_flag,
                            deps=[("sK_st", 16 * NGK), ("sV_st", 16 * NGV),
                                  ("sPar", 1), ("sFl", 16)],
                            siminc=("sFl", 16))
                flush(block, pB12)

        if DO_B:
            qa_g = ctx.enter_context(nc.sbuf_tensor("sb_qa", [128, H * TOWN], BF16))
            sb_q = qa_g
            b_guards = []

            def balloc(name, shp, dt):
                g = nc.sbuf_tensor(name, shp, dt)
                t = g.__enter__()
                b_guards.append(g)
                return t

            sb_xnoT = balloc("sb_xnoT", [128, NCH * TOWN], BF16)
            sb_bx = balloc("sb_bx", [128, D], F32)
            sb_bsq = balloc("sb_bsq", [128, D], BF16)
            sb_bst = balloc("sb_bst", [128, 16], F32)
            sb_rwT = balloc("sb_rwT", [128, NCH * NEXP], BF16)
            sb_AaT = balloc("sb_AaT", [128, NCH * 128], BF16)
            sb_wq0 = balloc("sb_wq0", [128, NCH * 128], BF16)
            sb_wq1 = balloc("sb_wq1", [128, NCH * 128], BF16)
            sb_lg = balloc("sb_lg", [128, 16], F32)
            sb_pp = balloc("sb_pp", [128, 16], F32)
            sb_pr = balloc("sb_pr", [128, 16], F32)
            sb_G0 = balloc("sb_G0", [128, 16], F32)
            sb_G1 = balloc("sb_G1", [128, 16], F32)
            sb_rank = balloc("sb_rank", [128, 16], F32)
            sb_msk = balloc("sb_msk", [128, 16], F32)
            sb_pm = balloc("sb_pm", [128, 16], F32)
            sb_pmT = balloc("sb_pmT", [NEXP, TOWN], BF16)
            sb_wexp = balloc("sb_wexp", [128, TOWN], F32)
            sb_emi = balloc("sb_emi", [NEXP, 1], I32)
            sb_emf = balloc("sb_emf", [NEXP, 1], F32)
            sb_onesB = balloc("sb_onesB", [NEXP, 16], F32)
            sb_te = balloc("sb_te", [NEXP, 1], F32)
            sb_M = balloc("sb_M", [NEXP, 128], BF16)
            sb_rbbf = balloc("sb_rbbf", [1, NEXP], BF16)
            sb_G = [sb_G0, sb_G1]
            sb_wqs = [sb_wq0, sb_wq1]

            FENCE_A12 = [("sK_g", NGK), ("sV_g", NGV), ("sK_st", 16 * NGK),
                         ("sV_st", 16 * NGV), ("sK_ev", NGK), ("sV_ev", NGV),
                         ("sK_w", 16 * NPASS * NCH), ("sV_w", 16 * NPASS * NEB),
                         ("sP_x", 16 * NPASS)]
            pB = Prog(sems, fence=FENCE_A12)

            def bst(eng, emit, deps=(), inc=None, inck=1):
                pB.st(eng, emit, deps, inc, inck)
                if inc:
                    cnt[inc] = cnt.get(inc, 0) + inck
                return cnt.get(inc, 0)

            bst("sync", lambda E: E.dma_start(
                    out=sb_emi[:, :], in_=emd.rearrange("(n o) -> n o", o=1)),
                inc="sM_c", inck=16)
            bst("gpsimd", lambda E: E.dma_start(
                    out=sb_rbbf[:, :], in_=rbd.rearrange("(o n) -> o n", o=1)),
                inc="sM_g", inck=16)
            bst("gpsimd", lambda E: E.dma_start(
                    out=sb_rwT.ap().rearrange("p (c n) -> p c n", n=NEXP),
                    in_=rWT.rearrange("(c p) n -> p c n", p=128)),
                inc="sM_g", inck=16)
            bst("gpsimd", lambda E: E.dma_start(
                    out=sb_AaT.ap().rearrange("p (c r) -> p c r", r=128),
                    in_=AaTd.rearrange("(c p) r -> p c r", p=128)),
                inc="sM_g", inck=16)
            bst("vector", lambda E: E.tensor_copy(out=sb_emf[:, :], in_=sb_emi[:, :]),
                deps=[("sM_c", cnt["sM_c"])], inc="sB_v")
            bst("vector", lambda E: E.memset(sb_onesB[:, :], 2.0), inc="sB_v")
            for e8 in range(8):
                bst("vector", lambda E, e8=e8: E.tensor_single_scalar(
                        out=sb_te[:, :], in_=sb_emf[:, :], scalar=float(e8),
                        op=ALU.is_equal),
                    deps=[("sB_v", cnt["sB_v"])], inc="sB_v")
                bst("vector", lambda E, e8=e8: E.tensor_scalar(
                        out=sb_M[:, e8 * 16:(e8 + 1) * 16], in0=sb_onesB[:, :],
                        scalar1=sb_te[:, 0:1], scalar2=None, op0=ALU.mult),
                    deps=[("sB_v", cnt["sB_v"])], inc="sB_v")

            lnv0 = cnt["sB_v"]
            for i in range(8):
                base_v, base_a = cnt["sB_v"], cnt["sB_a"]
                st_ = lambda k: sb_bst[:, k:k + 1]
                M1, S2, Uc, SD, RS, MP, Qc = 0, 1, 2, 3, 4, 5, 6
                bst("sync", lambda E, i=i: E.dma_start(
                        out=sb_bx[:, :], in_=xo[i * 128:(i + 1) * 128, :]),
                    deps=[("sB_v", base_v)], inc="sB_x", inck=16)
                bst("vector", lambda E, st_=st_: E.reduce_sum(st_(M1), sb_bx[:, :], axis=AX.X),
                    deps=[("sB_x", cnt["sB_x"]), ("sB_a", base_a)], inc="sB_v")
                bst("scalar", lambda E: E.activation(out=sb_bsq[:, :], in_=sb_bx[:, :],
                                                     func=AF.Square),
                    deps=[("sB_x", cnt["sB_x"]), ("sB_v", base_v),
                          ("sB_tp", cnt["sB_tp"])], inc="sB_a")
                bst("scalar", lambda E, st_=st_: E.activation(out=st_(Qc), in_=st_(M1),
                                                              func=AF.Square),
                    deps=[("sB_v", cnt["sB_v"])], inc="sB_a")
                bst("vector", lambda E, st_=st_: E.reduce_sum(st_(S2), sb_bsq[:, :], axis=AX.X),
                    deps=[("sB_a", cnt["sB_a"])], inc="sB_v")
                bst("vector", lambda E, st_=st_: E.scalar_tensor_tensor(
                        out=st_(Uc), in0=st_(Qc), scalar=-1.0 / D, in1=st_(S2),
                        op0=ALU.mult, op1=ALU.add),
                    deps=[("sB_v", cnt["sB_v"]), ("sB_a", cnt["sB_a"])], inc="sB_v")
                bst("scalar", lambda E, st_=st_: E.activation(
                        out=st_(SD), in_=st_(Uc), func=AF.Sqrt,
                        bias=sb_cst[:, 1:2], scale=sb_cst[:, 0:1]),
                    deps=[("sB_v", cnt["sB_v"])], inc="sB_a")
                bst("vector", lambda E, st_=st_: E.reciprocal(st_(RS), st_(SD)),
                    deps=[("sB_a", cnt["sB_a"])], inc="sB_v")
                bst("vector", lambda E, st_=st_: E.scalar_tensor_tensor(
                        out=st_(MP), in0=st_(M1), scalar=1.0 / D, in1=st_(RS),
                        op0=ALU.mult, op1=ALU.mult),
                    deps=[("sB_v", cnt["sB_v"])], inc="sB_v")
                bst("vector", lambda E, st_=st_: E.tensor_scalar(
                        out=sb_bsq[:, :], in0=sb_bx[:, :], scalar1=st_(RS), scalar2=st_(MP),
                        op0=ALU.mult, op1=ALU.subtract),
                    deps=[("sB_v", cnt["sB_v"]), ("sB_tp", cnt["sB_tp"])], inc="sB_v")
                xnv = cnt["sB_v"]
                for g in range(16):
                    half = g % 2

                    def em_tp(E, g=g, half=half):
                        ins = None
                        for k in range(2):
                            c = 2 * g + k
                            ins = E.transpose(
                                pt[:, half * 1024 + k * 128: half * 1024 + (k + 1) * 128],
                                sb_bsq[:, c * 128:(c + 1) * 128], sb_ident[:, :])
                        return ins
                    bst("tensor", em_tp,
                        deps=[("sB_v", xnv), ("sB_ev", cnt["sB_ev"] - 1),
                              ("sA_ev", 16 * NT_KV)],
                        inc="sB_tp")

                    def em_ev(E, i=i, g=g, half=half):
                        ins = None
                        for k in range(2):
                            c = 2 * g + k
                            ins = E.copy(
                                out=sb_xnoT[:, c * TOWN + i * 128: c * TOWN + (i + 1) * 128],
                                in_=pt[:, half * 1024 + k * 128: half * 1024 + (k + 1) * 128])
                        return ins
                    bst("scalar", em_ev, deps=[("sB_tp", cnt["sB_tp"])], inc="sB_ev")

            for i in range(8):
                prev_lg_ev = cnt["sB_a"]

                def em_rt(E, i=i):
                    reg = pS[:, 0:NEXP]
                    ins = None
                    for d_ in range(NCH):
                        ins = E.matmul(reg,
                                       sb_xnoT[:, d_ * TOWN + i * 128:
                                               d_ * TOWN + (i + 1) * 128],
                                       sb_rwT[:, d_ * NEXP:(d_ + 1) * NEXP],
                                       start=(d_ == 0), stop=False)
                    ins = E.matmul(reg, sb_ones[0:1, 0:128], sb_rbbf[0:1, :],
                                   start=False, stop=True)
                    return ins
                bst("tensor", em_rt,
                    deps=[("sB_ev", 16 * (i + 1)), ("sB_a", prev_lg_ev),
                          ("sK_ev", NGK), ("sM_g", 48), ("sQ_e", 0)],
                    inc="sB_g")
                bst("scalar", lambda E: E.copy(out=sb_lg[:, 0:NEXP], in_=pS[:, 0:NEXP]),
                    deps=[("sB_g", cnt["sB_g"])], inc="sB_a")
                bst("vector", lambda E: E.reduce_max(sb_bst[:, 7:8], sb_lg[:, 0:NEXP],
                                                     axis=AX.X),
                    deps=[("sB_a", cnt["sB_a"])], inc="sB_v")
                bst("vector", lambda E: E.tensor_scalar_mul(sb_bst[:, 8:9], sb_bst[:, 7:8],
                                                            -1.0),
                    deps=[("sB_v", cnt["sB_v"])], inc="sB_v")
                bst("scalar", lambda E: E.activation(out=sb_pp[:, 0:NEXP],
                                                     in_=sb_lg[:, 0:NEXP], func=AF.Exp,
                                                     bias=sb_bst[:, 8:9], scale=1.0),
                    deps=[("sB_v", cnt["sB_v"])], inc="sB_a")
                bst("vector", lambda E: E.reduce_sum(sb_bst[:, 9:10], sb_pp[:, 0:NEXP],
                                                     axis=AX.X),
                    deps=[("sB_a", cnt["sB_a"])], inc="sB_v")
                bst("vector", lambda E: E.reciprocal(sb_bst[:, 10:11], sb_bst[:, 9:10]),
                    deps=[("sB_v", cnt["sB_v"])], inc="sB_v")
                bst("vector", lambda E: E.tensor_scalar(
                        out=sb_pr[:, 0:NEXP], in0=sb_pp[:, 0:NEXP],
                        scalar1=sb_bst[:, 10:11], scalar2=None, op0=ALU.mult),
                    deps=[("sB_v", cnt["sB_v"])], inc="sB_v")
                prv = cnt["sB_v"]
                for n in range(NEXP):
                    bst("vector", lambda E, n=n: E.tensor_scalar(
                            out=sb_G[n % 2][:, 0:NEXP], in0=sb_pr[:, 0:NEXP],
                            scalar1=sb_pr[:, n:n + 1], scalar2=None, op0=ALU.is_gt),
                        deps=[("sB_v", prv), ("sB_p", cnt["sB_p"] - 1)], inc="sB_v")
                    bst("vector", lambda E, n=n: E.reduce_sum(
                            sb_rank[:, n:n + 1], sb_G[n % 2][:, 0:NEXP], axis=AX.X),
                        deps=[("sB_v", cnt["sB_v"])], inc="sB_p")
                bst("vector", lambda E: E.tensor_single_scalar(
                        out=sb_msk[:, 0:NEXP], in_=sb_rank[:, 0:NEXP], scalar=7.5,
                        op=ALU.is_le),
                    deps=[("sB_p", cnt["sB_p"])], inc="sB_v")
                bst("vector", lambda E: E.tensor_tensor(
                        out=sb_pm[:, 0:NEXP], in0=sb_pr[:, 0:NEXP], in1=sb_msk[:, 0:NEXP],
                        op=ALU.mult),
                    deps=[("sB_v", cnt["sB_v"])], inc="sB_v")
                bst("tensor", lambda E: E.transpose(po[0:NEXP, 0:128], sb_pm[:, 0:NEXP],
                                                    sb_identf[:, :]),
                    deps=[("sB_v", cnt["sB_v"]), ("sB_a", prev_lg_ev),
                      ("sV_ev", NGV)], inc="sB_g")
                bst("scalar", lambda E, i=i: E.copy(
                        out=sb_pmT[:, i * 128:(i + 1) * 128], in_=po[0:NEXP, 0:128]),
                    deps=[("sB_g", cnt["sB_g"])], inc="sB_a")
            pmT_done = cnt["sB_a"]

            for hf in range(2):
                bst("tensor", lambda E, hf=hf: E.matmul(
                        pS[:, hf * 512:(hf + 1) * 512], sb_M[:, 0:128],
                        sb_pmT[:, hf * 512:(hf + 1) * 512], start=True, stop=True),
                    deps=[("sB_a", pmT_done), ("sB_v", cnt["sB_v"])], inc="sB_g")
                bst("vector", lambda E, hf=hf: E.tensor_copy(
                        out=sb_wexp[:, hf * 512:(hf + 1) * 512],
                        in_=pS[:, hf * 512:(hf + 1) * 512]),
                    deps=[("sB_g", cnt["sB_g"])], inc="sB_v")
            wexp_done = cnt["sB_v"]

            for hf in range(2):
                def em_h(E, hf=hf):
                    reg = pS[:, hf * 512:(hf + 1) * 512]
                    ins = None
                    for d_ in range(NCH):
                        ins = E.matmul(reg,
                                       sb_AaT[:, d_ * 128:(d_ + 1) * 128],
                                       sb_xnoT[:, d_ * TOWN + hf * 512:
                                               d_ * TOWN + hf * 512 + 512],
                                       start=(d_ == 0), stop=(d_ == NCH - 1))
                    return ins
                bst("tensor", em_h,
                    deps=[("sB_ev", 128), ("sV_ev", NGV), ("sM_g", 48)], inc="sB_g")
                bst("vector", lambda E, hf=hf: E.scalar_tensor_tensor(
                        out=sb_hw[:, hf * 512:(hf + 1) * 512],
                        in0=pS[:, hf * 512:(hf + 1) * 512], scalar=1.0,
                        in1=sb_wexp[:, hf * 512:(hf + 1) * 512],
                        op0=ALU.mult, op1=ALU.mult),
                    deps=[("sB_g", cnt["sB_g"]), ("sB_v", wexp_done)], inc="sB_v")
            hw_done = cnt["sB_v"]

            wqTr = wqT.rearrange("(c p) e -> p c e", p=128)
            for h in range(H):
                bst("gpsimd", lambda E, h=h: E.dma_start(
                        out=sb_wqs[h % 2].ap().rearrange("p (c e) -> p c e", e=128),
                        in_=wqTr[:, :, h * 128:(h + 1) * 128]),
                    deps=[("sQ_g", h - 1)], inc="sQ_w", inck=16)

                def em_q(E, h=h):
                    ins = None
                    for hf in range(2):
                        reg = pS[:, hf * 512:(hf + 1) * 512]
                        for d_ in range(NCH):
                            ins = E.matmul(reg,
                                           sb_wqs[h % 2][:, d_ * 128:(d_ + 1) * 128],
                                           sb_xnoT[:, d_ * TOWN + hf * 512:
                                                   d_ * TOWN + hf * 512 + 512],
                                           start=(d_ == 0), stop=(d_ == NCH - 1))
                    return ins
                bst("tensor", em_q,
                    deps=[("sQ_w", 16 * (h + 1)), ("sQ_e", h), ("sB_v", wexp_done),
                          ("sB_ev", 128)],
                    inc="sQ_g")
                bst("scalar", lambda E, h=h: E.copy(
                        out=sb_q[:, h * TOWN:(h + 1) * TOWN], in_=pS[:, 0:1024]),
                    deps=[("sQ_g", h + 1)], inc="sQ_e")

            bst("sync", lambda E: E.dma_start(out=xnoT_d[:, :], in_=sb_xnoT[:, :]),
                deps=[("sB_ev", 128)], inc="sB_x", inck=16)
            spill_done = cnt["sB_x"]

            if debug:
                bst("sync", lambda E: E.dma_start(out=dbg["dbg_xnoT"], in_=sb_xnoT[:, :]),
                    deps=[("sB_ev", 128)], inc="sB_x", inck=16)
                bst("sync", lambda E: E.dma_start(out=dbg["dbg_q"], in_=sb_q[:, :]),
                    deps=[("sQ_e", H)], inc="sB_x", inck=16)
                bst("sync", lambda E: E.dma_start(out=dbg["dbg_hw"], in_=sb_hw[:, :]),
                    deps=[("sB_v", hw_done)], inc="sB_x", inck=16)
                bst("sync", lambda E: E.dma_start(out=dbg["dbg_wexp"], in_=sb_wexp[:, :]),
                    deps=[("sB_v", hw_done)], inc="sB_x", inck=16)
            flush(block, pB)
            for g in reversed(b_guards):
                g.__exit__(None, None, None)

        if DO_C:
            sb_attT = sb_q
            c_guards = []

            def calloc(name, shp, dt):
                g = nc.sbuf_tensor(name, shp, dt)
                t = g.__enter__()
                c_guards.append(g)
                return t

            sb_k0 = calloc("sb_k0", [128, 4096], BF16)
            sb_k1 = calloc("sb_k1", [128, 4096], BF16)
            sb_v0 = calloc("sb_v0", [128, 32 * 129], BF16)
            sb_v1 = calloc("sb_v1", [128, 32 * 129], BF16)
            sb_p0 = calloc("sb_p0", [128, 1024], BF16)
            sb_p1 = calloc("sb_p1", [128, 1024], BF16)
            sb_as0 = calloc("sb_as0", [128, 128], BF16)
            sb_as1 = calloc("sb_as1", [128, 128], BF16)
            sb_rl = calloc("sb_rl", [128, 8], F32)
            sb_ks = [sb_k0, sb_k1]
            sb_vs = [sb_v0, sb_v1]
            sb_ps = [sb_p0, sb_p1]
            sb_ass = [sb_as0, sb_as1]

            FENCE_B = [("sB_v", cnt["sB_v"]), ("sB_a", cnt["sB_a"]),
                       ("sB_p", cnt["sB_p"]), ("sB_g", cnt["sB_g"]),
                       ("sB_x", cnt["sB_x"]), ("sB_ev", cnt["sB_ev"]),
                       ("sB_tp", cnt["sB_tp"]), ("sQ_w", 16 * H),
                       ("sQ_g", H), ("sQ_e", H), ("sM_c", cnt["sM_c"]),
                       ("sM_g", cnt["sM_g"])]
            pC = Prog(sems, fence=FENCE_B)
            if pair:
                def _poll_slot(E, slot):
                    nonce = E.value_load(sb_meta[0:1, 0:1])

                    def cond():
                        E.sem_clear(sems["sPoll"])
                        E.dma_start(out=sb_fl[:, :],
                                    in_=shfl[slot, 0:1]).then_inc(sems["sPoll"], 16)
                        E.wait_ge(sems["sPoll"], 16)
                        vv = E.value_load(sb_fl[0:1, 0:1])
                        return vv != nonce
                    with E.While(cond):
                        E.nop()

                def em_poll(E):
                    pr = get_preg(E, 1)
                    with E.If_eq(pr, 0):
                        _poll_slot(E, 0)
                    with E.Else():
                        _poll_slot(E, 1)
                pC.st("sync", em_poll,
                      deps=[("sPar", 1), ("sFl", 16), ("sK_st", 16 * NGK),
                            ("sV_st", 16 * NGV)])
            pC.st("vector", lambda E: E.memset(sb_v0[:, :], 1.0), inc="sB_v")
            cnt["sB_v"] += 1
            pC.st("vector", lambda E: E.memset(sb_v1[:, :], 1.0), inc="sB_v")
            cnt["sB_v"] += 1
            memset_v = cnt["sB_v"]

            kvVr = kvV.rearrange("(a j t) e -> a t j e", t=128, j=32)

            def emit_pv(pC_, ks):
                u = ks // 32
                j = ks % 32
                h = ks // 64

                def em_pv(E, u=u, j=j, ks=ks):
                    ins = None
                    for qb in range(8):
                        ins = E.matmul(
                            po[:, PO_OFF[qb]:PO_OFF[qb] + 129],
                            sb_ps[ks % 2][:, qb * 128:(qb + 1) * 128],
                            sb_vs[u % 2][:, j * 129:(j + 1) * 129],
                            start=(ks % 64 == 0 and qb in PO_START),
                            stop=(ks % 64 == 63),
                            skip_group_check=True)
                    return ins
                pC_.st("tensor", em_pv,
                       deps=[("sC_e", 2 * ks + 2), ("sC_am", 8 * h),
                             ("sB_a", pmT_done)],
                       inc="sC_pv")
            B_FENCE_PE = [("sQ_e", H), ("sB_v", hw_done), ("sB_g", cnt["sB_g"]),
                          ("sB_a", cnt["sB_a"])]
            for h in range(H):
                for kk in range(2):
                    u = 2 * h + kk
                    pC.st("sync", lambda E, h=h, kk=kk, u=u: E.dma_start(
                            out=sb_ks[u % 2][:, :],
                            in_=kvK[h * 128:(h + 1) * 128, kk * 4096:(kk + 1) * 4096]),
                          deps=[("sC_pv", 32 * (u - 1)), ("sK_st", 16 * NGK),
                                ("sV_st", 16 * NGV)],
                          inc="sC_kv", inck=16)
                    pC.st("sync", lambda E, h=h, kk=kk, u=u: E.dma_start(
                            out=sb_vs[u % 2].ap().rearrange("p (j c) -> p j c", c=129)
                                [:, :, 0:128],
                            in_=kvVr[kk, :, :, h * 128:(h + 1) * 128]),
                          deps=[("sB_v", memset_v)],
                          inc="sC_kv", inck=16)
                    for j in range(32):
                        ks = u * 32 + j
                        for hf in range(2):
                            ui = 2 * ks + hf
                            reg = [pS[:, 0:512], pS[:, 512:1024],
                                   psE[:, 0:512]][ui % 3]

                            def em_sth(E, h=h, u=u, j=j, hf=hf, reg=reg):
                                return E.matmul(
                                    reg,
                                    sb_ks[u % 2][:, j * 128:(j + 1) * 128],
                                    sb_q[:, h * TOWN + hf * 512:
                                         h * TOWN + hf * 512 + 512],
                                    start=True, stop=True)
                            pC.st("tensor", em_sth,
                                  deps=[("sC_kv", 32 * (u + 1)),
                                        ("sC_e", ui - 2),
                                        ("sQ_e", H)] + B_FENCE_PE,
                                  inc="sC_s")

                            def em_exph(E, ks=ks, hf=hf, reg=reg):
                                return E.activation(
                                    out=sb_ps[ks % 2][:, hf * 512:(hf + 1) * 512],
                                    in_=reg,
                                    func=AF.Exp, bias=sb_cst[:, 3:4],
                                    scale=sb_cst[:, 2:3])
                            pC.st("scalar", em_exph,
                                  deps=[("sC_s", ui + 1),
                                        ("sC_pv", ks - 1), ("sB_v", CSTV)],
                                  inc="sC_e")
                        if ks % 64 > 0:
                            emit_pv(pC, ks - 1)
                emit_pv(pC, 64 * h + 63)

                def em_rl(E, h=h):
                    ins = None
                    for qb in range(8):
                        ins = E.reciprocal(sb_rl[:, qb:qb + 1],
                                           po[:, PO_OFF[qb] + 128:PO_OFF[qb] + 129])
                    return ins
                pC.st("vector", em_rl, deps=[("sC_pv", 64 * (h + 1))], inc="sC_rl")
                for qb in range(8):
                    pC.st("vector", lambda E, h=h, qb=qb: E.tensor_scalar(
                            out=sb_ass[qb % 2][:, :],
                            in0=po[:, PO_OFF[qb]:PO_OFF[qb] + 128],
                            scalar1=sb_rl[:, qb:qb + 1], scalar2=None, op0=ALU.mult),
                          deps=[("sC_rl", h + 1), ("sC_tr", 8 * h + qb - 1)],
                          inc="sC_am")
                    pC.st("tensor", lambda E, h=h, qb=qb: E.transpose(
                            pt[:, (qb % 2) * 1024:(qb % 2) * 1024 + 128],
                            sb_ass[qb % 2][:, :], sb_ident[:, :]),
                          deps=[("sC_am", 8 * h + qb + 1), ("sC_ae", 8 * h + qb - 1)],
                          inc="sC_tr")
                    pC.st("scalar", lambda E, h=h, qb=qb: E.copy(
                            out=sb_attT[:, h * TOWN + qb * 128: h * TOWN + (qb + 1) * 128],
                            in_=pt[:, (qb % 2) * 1024:(qb % 2) * 1024 + 128]),
                          deps=[("sC_tr", 8 * h + qb + 1), ("sC_s", 128 * (h + 1))],
                          inc="sC_ae")
            if debug:
                pC.st("sync", lambda E: E.dma_start(out=dbg["dbg_attT"], in_=sb_attT[:, :]),
                      deps=[("sC_ae", 8 * H)], inc="sC_kv", inck=16)
                pC.st("sync", lambda E: E.dma_start(
                        out=dbg["dbg_k"][0:256, :], in_=kvK[0:256, :]),
                      deps=[("sK_st", 16 * NGK)], inc="sC_kv", inck=16)
                pC.st("sync", lambda E: E.dma_start(
                        out=dbg["dbg_k"][256:384, :], in_=kvK[17 * 128:18 * 128, :]),
                      deps=[("sK_st", 16 * NGK)], inc="sC_kv", inck=16)
                pC.st("sync", lambda E: E.dma_start(
                        out=dbg["dbg_v"][:, :], in_=kvV[0:512, :]),
                      deps=[("sV_st", 16 * NGV)], inc="sC_kv", inck=16)
            flush(block, pC)
            for g in reversed(c_guards):
                g.__exit__(None, None, None)

        if DO_D:
            with (
                nc.sbuf_tensor("sb_xnoT2", [128, NCH * TOWN], BF16) as sb_xnoT2,
                nc.sbuf_tensor("sb_w10", [128, NCH * OB], BF16) as w10,
                nc.sbuf_tensor("sb_w11", [128, NCH * OB], BF16) as w11,
                nc.sbuf_tensor("sb_w20", [128, NCH * OB], BF16) as w20,
                nc.sbuf_tensor("sb_w21", [128, NCH * OB], BF16) as w21,
                nc.sbuf_tensor("sb_BaT", [128, D], BF16) as sb_BaT,
                nc.sbuf_tensor("sb_bbf", [1, D], BF16) as sb_bbf,
                nc.sbuf_tensor("sb_xr0", [128, OB], F32) as xr0,
                nc.sbuf_tensor("sb_xr1", [128, OB], F32) as xr1,
            ):
                w1s, w2s, xrs = [w10, w11], [w20, w21], [xr0, xr1]
                FENCE_C = [("sC_pv", 64 * H), ("sC_e", 128 * H), ("sC_s", 128 * H),
                           ("sC_ae", 8 * H), ("sC_am", 8 * H), ("sC_tr", 8 * H),
                           ("sC_rl", H),
                           ("sC_kv", 32 * 2 * H + (64 if debug else 0))]
                pD = Prog(sems, fence=FENCE_C)
                pD.st("gpsimd", lambda E: E.dma_start(out=sb_BaT[:, :], in_=BaTd[:, :]),
                      inc="sD_w", inck=16)
                pD.st("gpsimd", lambda E: E.dma_start(
                        out=sb_bbf[:, :], in_=bbd.rearrange("(o n) -> o n", o=1)),
                      inc="sD_w", inck=16)
                pD.st("sync", lambda E: E.dma_start(out=sb_xnoT2[:, :], in_=xnoT_d[:, :]),
                      deps=[("sB_x", spill_done)], inc="sM_c", inck=16)
                cnt["sM_c"] += 16
                xno2_done = cnt["sM_c"]
                woTr = woT.rearrange("(c p) e -> p c e", p=128)
                bWTr = bWT.rearrange("(c p) e -> p c e", p=128)
                DW0 = 32
                for ob in range(NOB):
                    pD.st("gpsimd", lambda E, ob=ob: E.dma_start(
                            out=w1s[ob % 2].ap().rearrange("p (c e) -> p c e", e=OB),
                            in_=woTr[:, :, ob * OB:(ob + 1) * OB]),
                          deps=[("sD_m", 8 * (ob - 1))], inc="sD_w", inck=16)
                    pD.st("gpsimd", lambda E, ob=ob: E.dma_start(
                            out=w2s[ob % 2].ap().rearrange("p (c e) -> p c e", e=OB),
                            in_=bWTr[:, :, ob * OB:(ob + 1) * OB]),
                          deps=[("sD_m", 8 * (ob - 1))], inc="sD_w", inck=16)
                    for tt in range(8):
                        bidx = ob * 8 + tt
                        psD_off = 0 if bidx % 2 == 0 else 512
                        pD.st("sync", lambda E, ob=ob, tt=tt, bidx=bidx: E.dma_start(
                                out=xrs[bidx % 2][:, :],
                                in_=xo[tt * 128:(tt + 1) * 128, ob * OB:(ob + 1) * OB]),
                              deps=[("sD_o", 16 * (bidx - 1)), ("sM_c", xno2_done)],
                              inc="sD_x", inck=16)

                        def em_blk(E, ob=ob, tt=tt, bidx=bidx, psD_off=psD_off):
                            reg = pS[:, psD_off:psD_off + OB]
                            ins = None
                            for e_ in range(NCH):
                                ins = E.matmul(
                                    reg,
                                    sb_attT[:, e_ * TOWN + tt * 128:
                                            e_ * TOWN + (tt + 1) * 128],
                                    w1s[ob % 2][:, e_ * OB:(e_ + 1) * OB],
                                    start=(e_ == 0), stop=False)
                            for d_ in range(NCH):
                                ins = E.matmul(
                                    reg,
                                    sb_xnoT2[:, d_ * TOWN + tt * 128:
                                             d_ * TOWN + (tt + 1) * 128],
                                    w2s[ob % 2][:, d_ * OB:(d_ + 1) * OB],
                                    start=False, stop=False)
                            ins = E.matmul(reg, sb_hw[:, tt * 128:(tt + 1) * 128],
                                           sb_BaT[:, ob * OB:(ob + 1) * OB],
                                           start=False, stop=False)
                            ins = E.matmul(reg, sb_ones[0:1, 0:128],
                                           sb_bbf[0:1, ob * OB:(ob + 1) * OB],
                                           start=False, stop=True)
                            return ins
                        pD.st("tensor", em_blk,
                              deps=[("sD_w", DW0 + 32 * (ob + 1)), ("sD_a", bidx - 1),
                                    ("sC_ae", 8 * H), ("sC_e", 128 * H),
                                    ("sB_v", hw_done), ("sM_c", xno2_done)],
                              inc="sD_m")
                        pD.st("vector", lambda E, bidx=bidx, psD_off=psD_off: E.tensor_tensor(
                                out=xrs[bidx % 2][:, :], in0=pS[:, psD_off:psD_off + OB],
                                in1=xrs[bidx % 2][:, :], op=ALU.add),
                              deps=[("sD_m", bidx + 1), ("sD_x", 16 * (bidx + 1)),
                                    ("sC_am", 8 * H)],
                              inc="sD_a")
                        pD.st("sync", lambda E, ob=ob, tt=tt, bidx=bidx: E.dma_start(
                                out=out[tt * 128:(tt + 1) * 128, ob * OB:(ob + 1) * OB],
                                in_=xrs[bidx % 2][:, :]),
                              deps=[("sD_a", bidx + 1)], inc="sD_o", inck=16)
                pD.st("sync", None, deps=[("sD_o", 16 * NBLK)])
                flush(block, pD)
    return nc


def _prep(inputs):
    g = lambda k: np.asarray(inputs[k])
    x = np.ascontiguousarray(g("L0_attn_x").reshape(S, D).astype(np.float32, copy=False))
    m = {
        "xf": x,
        "wqT": np.ascontiguousarray(g("L0_attn_w_q").T.astype(np.float32, copy=False)),
        "wkT": np.ascontiguousarray(g("L0_attn_w_k").T.astype(np.float32, copy=False)),
        "wvT": np.ascontiguousarray(g("L0_attn_w_v").T.astype(np.float32, copy=False)),
        "woT": np.ascontiguousarray(g("L0_attn_w_o").T.astype(np.float32, copy=False)),
        "bWT": np.ascontiguousarray(g("L0_ffn_base_W").T.astype(np.float32, copy=False)),
        "rWT": np.ascontiguousarray(g("L0_ffn_router_W").T.astype(np.float32, copy=False)),
        "AaT": np.ascontiguousarray(
            g("L0_ffn_A").transpose(2, 0, 1).reshape(D, 128).astype(np.float32, copy=False)),
        "BaT": np.ascontiguousarray(
            g("L0_ffn_B").transpose(0, 2, 1).reshape(128, D).astype(np.float32, copy=False)),
        "bb": np.ascontiguousarray(g("L0_ffn_base_b").astype(np.float32, copy=False)),
        "rb": np.ascontiguousarray(g("L0_ffn_router_b").astype(np.float32, copy=False)),
        "em": np.ascontiguousarray(g("L0_ffn_expert_map").astype(np.int32, copy=False)),
        "identbf": np.eye(128, dtype=BFD),
    }
    return x, m


def kernel(**inputs):
    x, shared = _prep(inputs)
    pair = True
    nc = build_nc(pair=pair)
    nonce = int(np.random.default_rng().integers(1, 2 ** 30))
    in_maps = []
    for c in range(NCORES):
        m = dict(shared)
        m["xo"] = np.ascontiguousarray(x[c * TOWN:(c + 1) * TOWN])
        if pair:
            m["meta"] = np.array([nonce, 0, 0, 0], dtype=np.int32)
        in_maps.append(m)
    res = run_bass_kernel_spmd(nc, in_maps, core_ids=list(range(NCORES)))
    full = np.concatenate([res.results[c]["out"] for c in range(NCORES)], axis=0)
    return full.reshape(1, S, D)


# version 09
# speedup vs baseline: 1.1249x; 1.1249x over previous
import contextlib

import numpy as np
import ml_dtypes
import concourse.bass as bass
import concourse.mybir as mybir
from concourse.bass_utils import run_bass_kernel_spmd

from concourse.compiler_utils import get_compiler_flags, set_compiler_flags

_flags = get_compiler_flags()
_patched = [f.replace("--enable-ldw-opt=false", "--enable-ldw-opt=true")
            for f in _flags]
if _patched != _flags:
    set_compiler_flags(_patched)

F32 = mybir.dt.float32
BF16 = mybir.dt.bfloat16
I32 = mybir.dt.int32
AF = mybir.ActivationFunctionType
ALU = mybir.AluOpType
AX = mybir.AxisListType
BFD = ml_dtypes.bfloat16

S, D, H, DK = 8192, 4096, 32, 128
NCORES = 8
TOWN = S // NCORES
NCH = D // 128
NEXP = 15
EPS = 1e-5
CSCL = float(1.0 / np.sqrt(DK))
M0 = 8.0
PASS_T = 1024
OB = 128
NOB = D // OB
NBLK = NOB * (TOWN // 128)

ENG_NAMES = ("sync", "gpsimd", "vector", "scalar", "tensor")


TRUNC = [None, 0]


class Prog:
    def __init__(self, sems, fence=()):
        self.sems = sems
        self.fence = [(s, v) for (s, v) in fence if v > 0]
        self.steps = []

    def st(self, eng, emit, deps=(), inc=None, inck=1, siminc=None):
        if TRUNC[0] is not None:
            if TRUNC[1] >= TRUNC[0]:
                return
            TRUNC[1] += 1
        self.steps.append((eng, [(s, v) for (s, v) in deps if v > 0], emit, inc,
                           inck, siminc))


ALL_STEPS = []


def flush(block, prog):
    if prog.fence:
        seen = set()
        for i, st_ in enumerate(prog.steps):
            if st_[0] not in seen:
                seen.add(st_[0])
                prog.steps[i] = (st_[0], list(prog.fence) + st_[1], *st_[2:])
    ALL_STEPS.extend(prog.steps)
    for name in ENG_NAMES:
        steps = [s for s in prog.steps if s[0] == name]
        if not steps:
            continue

        def run(E, steps=steps, sems=prog.sems):
            waited = {}
            for (_e, deps, emit, inc, inck, _si) in steps:
                for (s, v) in deps:
                    if waited.get(s, 0) < v:
                        E.wait_ge(sems[s], v)
                        waited[s] = v
                if emit is not None:
                    ins = emit(E)
                    if inc is not None:
                        ins.then_inc(sems[inc], inck)
        getattr(block, name)(run)


def build_nc(pair=False, debug=False, phases="abcd", kv_t=None):
    KV_T = kv_t if kv_t else (S // 2 if pair else S)
    NT_KV = KV_T // 128
    NPASS = KV_T // PASS_T
    TBK = PASS_T // 512
    NEB = D // 256
    NTT = PASS_T // 128
    NGK = NPASS * NCH * TBK
    NGV = NPASS * NEB * NTT

    nc = bass.Bass("TRN2", target_bir_lowering=False)

    xf = nc.dram_tensor("xf", [S, D], F32, kind="ExternalInput").ap()
    xo = nc.dram_tensor("xo", [TOWN, D], F32, kind="ExternalInput").ap()
    wqT = nc.dram_tensor("wqT", [D, D], F32, kind="ExternalInput").ap()
    wkT = nc.dram_tensor("wkT", [D, D], F32, kind="ExternalInput").ap()
    wvT = nc.dram_tensor("wvT", [D, D], F32, kind="ExternalInput").ap()
    woT = nc.dram_tensor("woT", [D, D], F32, kind="ExternalInput").ap()
    bWT = nc.dram_tensor("bWT", [D, D], F32, kind="ExternalInput").ap()
    rWT = nc.dram_tensor("rWT", [D, NEXP], F32, kind="ExternalInput").ap()
    AaTd = nc.dram_tensor("AaT", [D, 128], F32, kind="ExternalInput").ap()
    BaTd = nc.dram_tensor("BaT", [128, D], F32, kind="ExternalInput").ap()
    bbcold = nc.dram_tensor("bbcol", [128, NOB], F32, kind="ExternalInput").ap()
    rbd = nc.dram_tensor("rb", [NEXP], F32, kind="ExternalInput").ap()
    emd = nc.dram_tensor("em", [NEXP], I32, kind="ExternalInput").ap()
    identbf = nc.dram_tensor("identbf", [128, 128], BF16, kind="ExternalInput").ap()
    metad = nc.dram_tensor("meta", [4], I32, kind="ExternalInput").ap()
    xoTd = nc.dram_tensor("xoT", [D, TOWN], F32, kind="ExternalInput").ap()
    out = nc.dram_tensor("out", [D, TOWN], F32, kind="ExternalOutput").ap()

    sp = dict(addr_space="Shared") if pair else {}
    xnT_d = nc.dram_tensor("xnT_d", [NCH, 128, KV_T], BF16, kind="Internal").ap()
    kvK = nc.dram_tensor("kvK", [D, S], BF16, kind="Internal", **sp).ap()
    kvV = nc.dram_tensor("kvV", [S, D], BF16, kind="Internal", **sp).ap()
    xnoT_d = nc.dram_tensor("xnoT_d", [128, NCH * TOWN], BF16, kind="Internal").ap()
    shfl = (nc.dram_tensor("shfl", [2, 128], I32, kind="Internal",
                           addr_space="Shared").ap() if pair else None)

    dbg = {}
    if debug:
        for nm, shp, dt in [
            ("dbg_q", [128, H * TOWN], BF16),
            ("dbg_attT", [128, NCH * TOWN], BF16),
            ("dbg_hw", [128, TOWN], BF16),
            ("dbg_wexp", [128, TOWN], F32),
            ("dbg_xnoT", [128, NCH * TOWN], BF16),
            ("dbg_k", [384, S], BF16),
            ("dbg_v", [512, D], BF16),
        ]:
            dbg[nm] = nc.dram_tensor(nm, shp, dt, kind="ExternalOutput").ap()

    sem_names = [
        "sA_x", "sA_v", "sA_a", "sA_tp", "sA_ev", "sA_st",
        "sK_w", "sK_g", "sK_ev", "sK_st", "sP_x",
        "sV_w", "sV_g", "sV_ev", "sV_st",
        "sB_x", "sB_v", "sB_a", "sB_p", "sB_tp", "sB_ev", "sB_g",
        "sQ_w", "sQ_g", "sQ_e",
        "sC_kv", "sC_s", "sC_e", "sC_pv", "sC_rl", "sC_am", "sC_tr", "sC_ae",
        "sD_w", "sD_x", "sD_m", "sD_a", "sD_b", "sD_o", "sM_c", "sM_g",
        "sPar", "sFl", "sPoll",
    ]

    with contextlib.ExitStack() as ctx:
        sems = {n: ctx.enter_context(nc.semaphore(n)) for n in sem_names}
        sb_ident = ctx.enter_context(nc.sbuf_tensor("sb_ident", [128, 128], BF16))
        sb_identf = ctx.enter_context(nc.sbuf_tensor("sb_identf", [128, 128], F32))
        sb_ones = ctx.enter_context(nc.sbuf_tensor("sb_ones", [1, 128], BF16))
        sb_hw = ctx.enter_context(nc.sbuf_tensor("sb_hw", [128, TOWN], BF16))
        sb_cst = ctx.enter_context(nc.sbuf_tensor("sb_cst", [128, 4], F32))
        sb_par = ctx.enter_context(nc.sbuf_tensor("sb_par", [1, 2], I32))
        sb_meta = ctx.enter_context(nc.sbuf_tensor("sb_meta2", [1, 4], I32))
        sb_fl = ctx.enter_context(nc.sbuf_tensor("sb_fl", [1, 1], I32))
        pS = ctx.enter_context(nc.psum_tensor("pS", [128, 1024], F32))
        po = ctx.enter_context(nc.psum_tensor("po", [128, 1536], F32))
        pt = ctx.enter_context(nc.psum_tensor("pt", [128, 2048], BF16))
        PO_OFF = [0, 129, 258, 512, 641, 770, 1024, 1153]
        PO_START = {0, 3, 6}
        block = ctx.enter_context(nc.Block())

        par_regs = {}

        def get_preg(E, col=0):
            k = (str(E.engine), col)
            if k not in par_regs:
                par_regs[k] = E.alloc_register(f"parreg_{k[0]}_{col}")
                v = E.value_load(sb_par[0:1, col:col + 1])
                E.reg_mov(par_regs[k], v)
            return par_regs[k]

        def par_dma(E, emit0, emit1, inc_h=None, inck=16):
            pr = get_preg(E, 0)
            with E.If_eq(pr, 0):
                i0 = emit0(E)
                if inc_h is not None:
                    i0.then_inc(sems[inc_h], inck)
            with E.Else():
                i1 = emit1(E)
                if inc_h is not None:
                    i1.then_inc(sems[inc_h], inck)

        if pair:
            @block.gpsimd
            def _(g):
                for arm in g.switch_core_id(n=128):
                    parity = int(arm) & 1
                    g.memset(sb_par[0:1, 0:1], parity)
                    g.memset(sb_par[0:1, 1:2], 1 - parity).then_inc(sems["sPar"], 1)
        p0 = Prog(sems)
        if pair:
            p0.st("sync", lambda E: E.dma_start(
                    out=sb_meta[:, :], in_=metad.rearrange("(o n) -> o n", o=1)),
                  inc="sFl", inck=16)
        p0.st("sync", lambda E: E.dma_start(out=sb_ident[:, :], in_=identbf[:, :]),
              inc="sM_c", inck=16)
        p0.st("vector", lambda E: E.memset(sb_ones[:, :], 1.0), inc="sB_v")
        p0.st("vector", lambda E: E.tensor_copy(out=sb_identf[:, :], in_=sb_ident[:, :]),
              deps=[("sM_c", 16)], inc="sB_v")
        for ci, cv in enumerate([1.0 / D, EPS, CSCL, -M0]):
            p0.st("vector", lambda E, ci=ci, cv=cv: E.memset(sb_cst[:, ci:ci + 1], cv),
                  inc="sB_v")
        flush(block, p0)
        CSTV = 6
        cnt = {"sB_v": 6, "sB_a": 0, "sB_p": 0, "sB_g": 0, "sM_c": 16, "sM_g": 0,
               "sB_x": 0, "sB_tp": 0, "sB_ev": 0}

        DO_A = "a" in phases
        DO_B = "b" in phases
        DO_C = "c" in phases
        DO_D = "d" in phases
        if DO_A:
            with (
                nc.sbuf_tensor("sb_ax0", [128, D], F32) as ax0,
                nc.sbuf_tensor("sb_ax1", [128, D], F32) as ax1,
                nc.sbuf_tensor("sb_asq0", [128, D], BF16) as asq0,
                nc.sbuf_tensor("sb_asq1", [128, D], BF16) as asq1,
                nc.sbuf_tensor("sb_axn0", [128, D], BF16) as axn0,
                nc.sbuf_tensor("sb_axn1", [128, D], BF16) as axn1,
                nc.sbuf_tensor("sb_ast0", [128, D], BF16) as ast0,
                nc.sbuf_tensor("sb_ast1", [128, D], BF16) as ast1,
                nc.sbuf_tensor("sb_stat", [128, 16], F32) as sb_stat,
            ):
                axs, asqs, axns, asts = [ax0, ax1], [asq0, asq1], [axn0, axn1], [ast0, ast1]
                pA = Prog(sems)
                M1, S2, Uc, SD, RS, MP, Qc = 0, 1, 2, 3, 4, 5, 6

                def a0_ln(p, i, xin_rows, src, bufs, sems_pfx, extra_x_dep=()):
                    sx, sv, sa, stp = sems_pfx
                    b = i % 2
                    ax, asq, axn = bufs[0][b], bufs[1][b], bufs[2][b]
                    st_ = lambda k, b=b: sb_stat[:, b * 8 + k: b * 8 + k + 1]
                    if pair:
                        p.st("sync", lambda E, ax=ax: par_dma(
                                E,
                                lambda E2: E2.dma_start(
                                    out=ax[:, :],
                                    in_=src[xin_rows[0]:xin_rows[1], :]),
                                lambda E2: E2.dma_start(
                                    out=ax[:, :],
                                    in_=src[KV_T + xin_rows[0]:
                                            KV_T + xin_rows[1], :]),
                                inc_h=sx),
                             deps=[(sv, 6 * (i - 1)), ("sPar", 1)]
                                  + list(extra_x_dep), siminc=(sx, 16))
                    else:
                        p.st("sync", lambda E, ax=ax: E.dma_start(
                                out=ax[:, :], in_=src[xin_rows[0]:xin_rows[1], :]),
                             deps=[(sv, 6 * (i - 1))] + list(extra_x_dep),
                             inc=sx, inck=16)
                    p.st("vector", lambda E, ax=ax, st_=st_: E.reduce_sum(
                            st_(M1), ax[:, :], axis=AX.X),
                         deps=[(sx, 16 * (i + 1)), (sa, 3 * (i - 2) + 2)], inc=sv)
                    p.st("scalar", lambda E, ax=ax, asq=asq: E.activation(
                            out=asq[:, :], in_=ax[:, :], func=AF.Square),
                         deps=[(sx, 16 * (i + 1)), (sv, 6 * (i - 2) + 2)], inc=sa)
                    p.st("scalar", lambda E, st_=st_: E.activation(
                            out=st_(Qc), in_=st_(M1), func=AF.Square),
                         deps=[(sv, 6 * i + 1)], inc=sa)
                    p.st("vector", lambda E, asq=asq, st_=st_: E.reduce_sum(
                            st_(S2), asq[:, :], axis=AX.X),
                         deps=[(sa, 3 * i + 1)], inc=sv)
                    p.st("vector", lambda E, st_=st_: E.scalar_tensor_tensor(
                            out=st_(Uc), in0=st_(Qc), scalar=-1.0 / D, in1=st_(S2),
                            op0=ALU.mult, op1=ALU.add),
                         deps=[(sv, 6 * i + 2), (sa, 3 * i + 2)], inc=sv)
                    p.st("scalar", lambda E, st_=st_: E.activation(
                            out=st_(SD), in_=st_(Uc), func=AF.Sqrt,
                            bias=sb_cst[:, 1:2], scale=sb_cst[:, 0:1]),
                         deps=[(sv, 6 * i + 3), ("sB_v", CSTV)], inc=sa)
                    p.st("vector", lambda E, st_=st_: E.reciprocal(st_(RS), st_(SD)),
                         deps=[(sa, 3 * i + 3)], inc=sv)
                    p.st("vector", lambda E, st_=st_: E.scalar_tensor_tensor(
                            out=st_(MP), in0=st_(M1), scalar=1.0 / D, in1=st_(RS),
                            op0=ALU.mult, op1=ALU.mult),
                         deps=[(sv, 6 * i + 4)], inc=sv)
                    p.st("vector", lambda E, ax=ax, axn=axn, st_=st_: E.tensor_scalar(
                            out=axn[:, :], in0=ax[:, :], scalar1=st_(RS), scalar2=st_(MP),
                            op0=ALU.mult, op1=ALU.subtract),
                         deps=[(sv, 6 * i + 5), (stp, 16 * (i - 1))], inc=sv)

                xnTr = xnT_d.rearrange("c p t -> p c t")
                for i in range(NT_KV):
                    b = i % 2
                    a0_ln(pA, i, (i * 128, i * 128 + 128), xf, (axs, asqs, axns),
                          ("sA_x", "sA_v", "sA_a", "sA_tp"))
                    for g in range(16):
                        G = 16 * i + g
                        half = g % 2

                        def em_tp(E, b=b, g=g, half=half):
                            ins = None
                            for k in range(2):
                                c = 2 * g + k
                                ins = E.transpose(
                                    pt[:, half * 1024 + k * 128: half * 1024 + (k + 1) * 128],
                                    axns[b][:, c * 128:(c + 1) * 128], sb_ident[:, :])
                            return ins
                        pA.st("tensor", em_tp,
                              deps=[("sA_v", 6 * i + 6), ("sA_ev", G - 1)], inc="sA_tp")

                        def em_ev(E, b=b, g=g, half=half):
                            return E.copy(out=asts[b][:, (2 * g) * 128:(2 * g + 2) * 128],
                                          in_=pt[:, half * 1024: half * 1024 + 256])
                        pA.st("scalar", em_ev,
                              deps=[("sA_tp", G + 1), ("sA_st", 16 * (i - 1))], inc="sA_ev")

                    def em_st(E, b=b, i=i):
                        return E.dma_start(
                            out=xnTr[:, :, i * 128:(i + 1) * 128],
                            in_=asts[b].ap().rearrange("p (c t) -> p c t", t=128))
                    pA.st("sync", em_st, deps=[("sA_ev", 16 * (i + 1))], inc="sA_st", inck=16)
                flush(block, pA)

            with (
                nc.sbuf_tensor("sb_xnp", [128, NCH * PASS_T], BF16) as sb_xnp,
                nc.sbuf_tensor("sb_kw0", [128, NCH * 128], BF16) as kw0,
                nc.sbuf_tensor("sb_kw1", [128, NCH * 128], BF16) as kw1,
                nc.sbuf_tensor("sb_vw0", [128, NCH * 256], BF16) as vw0,
                nc.sbuf_tensor("sb_vw1", [128, NCH * 256], BF16) as vw1,
                nc.sbuf_tensor("sb_kst0", [128, 512], BF16) as kst0,
                nc.sbuf_tensor("sb_kst1", [128, 512], BF16) as kst1,
                nc.sbuf_tensor("sb_vst0", [128, 256], BF16) as vst0,
                nc.sbuf_tensor("sb_vst1", [128, 256], BF16) as vst1,
            ):
                kws, vws, ksts, vsts = [kw0, kw1], [vw0, vw1], [kst0, kst1], [vst0, vst1]
                wkTr = wkT.rearrange("(c p) e -> p c e", p=128)
                wvTr = wvT.rearrange("(c p) e -> p c e", p=128)
                xnp_r = sb_xnp.ap().rearrange("p (c t) -> p c t", t=PASS_T)
                FENCE_A0 = [("sA_st", 16 * NT_KV), ("sA_tp", 16 * NT_KV),
                            ("sA_ev", 16 * NT_KV), ("sA_v", 6 * NT_KV),
                            ("sA_a", 3 * NT_KV), ("sA_x", 16 * NT_KV)]
                pB12 = Prog(sems, fence=FENCE_A0)
                gk = 0
                gv = 0
                for p_ in range(NPASS):
                    def em_pl(E, p_=p_):
                        return E.dma_start(
                            out=xnp_r,
                            in_=xnTr[:, :, p_ * PASS_T:(p_ + 1) * PASS_T])
                    pB12.st("sync", em_pl,
                            deps=[("sA_st", 16 * (p_ + 1) * (PASS_T // 128)),
                                  ("sK_g", gk), ("sV_g", gv)],
                            inc="sP_x", inck=16)
                    for e in range(NCH):
                        L = p_ * NCH + e

                        def em_kw(E, L=L, e=e):
                            return E.dma_start(
                                out=kws[L % 2].ap().rearrange("p (c e) -> p c e", e=128),
                                in_=wkTr[:, :, e * 128:(e + 1) * 128])
                        pB12.st("gpsimd", em_kw, deps=[("sK_g", TBK * (L - 1))],
                                inc="sK_w", inck=16)
                        for tb in range(TBK):
                            def em_kg(E, L=L, e=e, tb=tb, p_=p_, gk=gk):
                                reg = pS[:, (gk % 2) * 512:(gk % 2) * 512 + 512]
                                ins = None
                                for d_ in range(NCH):
                                    ins = E.matmul(
                                        reg,
                                        kws[L % 2][:, d_ * 128:(d_ + 1) * 128],
                                        sb_xnp[:, d_ * PASS_T + tb * 512:
                                               d_ * PASS_T + tb * 512 + 512],
                                        start=(d_ == 0), stop=(d_ == NCH - 1))
                                return ins
                            pB12.st("tensor", em_kg,
                                    deps=[("sK_w", 16 * (L + 1)), ("sP_x", 16 * (p_ + 1)),
                                          ("sK_ev", gk - 1)],
                                    inc="sK_g")

                            def em_kev(E, gk=gk):
                                return E.copy(out=ksts[gk % 2][:, :],
                                              in_=pS[:, (gk % 2) * 512:(gk % 2) * 512 + 512])
                            pB12.st("scalar", em_kev,
                                    deps=[("sK_g", gk + 1), ("sK_st", 16 * (gk - 1))],
                                    inc="sK_ev")

                            colb = p_ * PASS_T + tb * 512

                            if pair:
                                def em_kst(E, gk=gk, e=e, colb=colb):
                                    par_dma(
                                        E,
                                        lambda E2: E2.dma_start(
                                            out=kvK[e * 128:(e + 1) * 128,
                                                    colb:colb + 512],
                                            in_=ksts[gk % 2][:, :]),
                                        lambda E2: E2.dma_start(
                                            out=kvK[e * 128:(e + 1) * 128,
                                                    KV_T + colb:KV_T + colb + 512],
                                            in_=ksts[gk % 2][:, :]),
                                        inc_h="sK_st")
                                pB12.st("scalar", em_kst,
                                        deps=[("sK_ev", gk + 1), ("sPar", 1)],
                                        siminc=("sK_st", 16))
                            else:
                                def em_kst(E, gk=gk, e=e, colb=colb):
                                    return E.dma_start(
                                        out=kvK[e * 128:(e + 1) * 128,
                                                colb:colb + 512],
                                        in_=ksts[gk % 2][:, :])
                                pB12.st("scalar", em_kst, deps=[("sK_ev", gk + 1)],
                                        inc="sK_st", inck=16)
                            gk += 1
                    for eb in range(NEB):
                        L2 = p_ * NEB + eb

                        def em_vw(E, L2=L2, eb=eb):
                            return E.dma_start(
                                out=vws[L2 % 2].ap().rearrange("p (c e) -> p c e", e=256),
                                in_=wvTr[:, :, eb * 256:(eb + 1) * 256])
                        pB12.st("gpsimd", em_vw, deps=[("sV_g", NTT * (L2 - 1))],
                                inc="sV_w", inck=16)
                        for tt in range(NTT):
                            def em_vg(E, L2=L2, tt=tt, p_=p_, gv=gv):
                                reg = po[:, (gv % 2) * 512:(gv % 2) * 512 + 256]
                                ins = None
                                for d_ in range(NCH):
                                    ins = E.matmul(
                                        reg,
                                        sb_xnp[:, d_ * PASS_T + tt * 128:
                                               d_ * PASS_T + tt * 128 + 128],
                                        vws[L2 % 2][:, d_ * 256:(d_ + 1) * 256],
                                        start=(d_ == 0), stop=(d_ == NCH - 1))
                                return ins
                            pB12.st("tensor", em_vg,
                                    deps=[("sV_w", 16 * (L2 + 1)), ("sP_x", 16 * (p_ + 1)),
                                          ("sV_ev", gv - 1)],
                                    inc="sV_g")

                            def em_vev(E, gv=gv):
                                return E.tensor_copy(
                                    out=vsts[gv % 2][:, :],
                                    in_=po[:, (gv % 2) * 512:(gv % 2) * 512 + 256])
                            pB12.st("vector", em_vev,
                                    deps=[("sV_g", gv + 1), ("sV_st", 16 * (gv - 1))],
                                    inc="sV_ev")

                            rowb = p_ * PASS_T + tt * 128

                            if pair:
                                def em_vst(E, gv=gv, eb=eb, rowb=rowb):
                                    par_dma(
                                        E,
                                        lambda E2: E2.dma_start(
                                            out=kvV[rowb:rowb + 128,
                                                    eb * 256:(eb + 1) * 256],
                                            in_=vsts[gv % 2][:, :]),
                                        lambda E2: E2.dma_start(
                                            out=kvV[KV_T + rowb:KV_T + rowb + 128,
                                                    eb * 256:(eb + 1) * 256],
                                            in_=vsts[gv % 2][:, :]),
                                        inc_h="sV_st")
                                pB12.st("scalar", em_vst,
                                        deps=[("sV_ev", gv + 1), ("sPar", 1)],
                                        siminc=("sV_st", 16))
                            else:
                                def em_vst(E, gv=gv, eb=eb, rowb=rowb):
                                    return E.dma_start(
                                        out=kvV[rowb:rowb + 128,
                                                eb * 256:(eb + 1) * 256],
                                        in_=vsts[gv % 2][:, :])
                                pB12.st("scalar", em_vst, deps=[("sV_ev", gv + 1)],
                                        inc="sV_st", inck=16)
                            gv += 1
                if pair:
                    def em_flag(E):
                        par_dma(
                            E,
                            lambda E2: E2.dma_start(out=shfl[0, 0:1],
                                                    in_=sb_meta[0:1, 0:1]),
                            lambda E2: E2.dma_start(out=shfl[1, 0:1],
                                                    in_=sb_meta[0:1, 0:1]),
                            inc_h="sFl")
                    pB12.st("gpsimd", em_flag,
                            deps=[("sK_st", 16 * NGK), ("sV_st", 16 * NGV),
                                  ("sPar", 1), ("sFl", 16)],
                            siminc=("sFl", 16))
                flush(block, pB12)

        if DO_B:
            qa_g = ctx.enter_context(nc.sbuf_tensor("sb_qa", [128, H * TOWN], BF16))
            sb_q = qa_g
            b_guards = []

            def balloc(name, shp, dt):
                g = nc.sbuf_tensor(name, shp, dt)
                t = g.__enter__()
                b_guards.append(g)
                return t

            sb_xnoT = balloc("sb_xnoT", [128, NCH * TOWN], BF16)
            sb_bx = balloc("sb_bx", [128, D], F32)
            sb_bsq = balloc("sb_bsq", [128, D], BF16)
            sb_bst = balloc("sb_bst", [128, 16], F32)
            sb_rwT = balloc("sb_rwT", [128, NCH * NEXP], BF16)
            sb_AaT = balloc("sb_AaT", [128, NCH * 128], BF16)
            sb_wq0 = balloc("sb_wq0", [128, NCH * 128], BF16)
            sb_wq1 = balloc("sb_wq1", [128, NCH * 128], BF16)
            sb_lg = balloc("sb_lg", [128, 16], F32)
            sb_pp = balloc("sb_pp", [128, 16], F32)
            sb_pr = balloc("sb_pr", [128, 16], F32)
            sb_G0 = balloc("sb_G0", [128, 16], F32)
            sb_G1 = balloc("sb_G1", [128, 16], F32)
            sb_rank = balloc("sb_rank", [128, 16], F32)
            sb_msk = balloc("sb_msk", [128, 16], F32)
            sb_pm = balloc("sb_pm", [128, 16], F32)
            sb_pmT = balloc("sb_pmT", [NEXP, TOWN], BF16)
            sb_wexp = balloc("sb_wexp", [128, TOWN], F32)
            sb_emi = balloc("sb_emi", [NEXP, 1], I32)
            sb_emf = balloc("sb_emf", [NEXP, 1], F32)
            sb_onesB = balloc("sb_onesB", [NEXP, 16], F32)
            sb_te = balloc("sb_te", [NEXP, 1], F32)
            sb_M = balloc("sb_M", [NEXP, 128], BF16)
            sb_rbbf = balloc("sb_rbbf", [1, NEXP], BF16)
            sb_G = [sb_G0, sb_G1]
            sb_wqs = [sb_wq0, sb_wq1]

            FENCE_A12 = [("sK_g", NGK), ("sV_g", NGV), ("sK_st", 16 * NGK),
                         ("sV_st", 16 * NGV), ("sK_ev", NGK), ("sV_ev", NGV),
                         ("sK_w", 16 * NPASS * NCH), ("sV_w", 16 * NPASS * NEB),
                         ("sP_x", 16 * NPASS)]
            pB = Prog(sems, fence=FENCE_A12)

            def bst(eng, emit, deps=(), inc=None, inck=1):
                pB.st(eng, emit, deps, inc, inck)
                if inc:
                    cnt[inc] = cnt.get(inc, 0) + inck
                return cnt.get(inc, 0)

            bst("sync", lambda E: E.dma_start(
                    out=sb_emi[:, :], in_=emd.rearrange("(n o) -> n o", o=1)),
                inc="sM_c", inck=16)
            bst("gpsimd", lambda E: E.dma_start(
                    out=sb_rbbf[:, :], in_=rbd.rearrange("(o n) -> o n", o=1)),
                inc="sM_g", inck=16)
            bst("gpsimd", lambda E: E.dma_start(
                    out=sb_rwT.ap().rearrange("p (c n) -> p c n", n=NEXP),
                    in_=rWT.rearrange("(c p) n -> p c n", p=128)),
                inc="sM_g", inck=16)
            bst("gpsimd", lambda E: E.dma_start(
                    out=sb_AaT.ap().rearrange("p (c r) -> p c r", r=128),
                    in_=AaTd.rearrange("(c p) r -> p c r", p=128)),
                inc="sM_g", inck=16)
            bst("vector", lambda E: E.tensor_copy(out=sb_emf[:, :], in_=sb_emi[:, :]),
                deps=[("sM_c", cnt["sM_c"])], inc="sB_v")
            bst("vector", lambda E: E.memset(sb_onesB[:, :], 2.0), inc="sB_v")
            for e8 in range(8):
                bst("vector", lambda E, e8=e8: E.tensor_single_scalar(
                        out=sb_te[:, :], in_=sb_emf[:, :], scalar=float(e8),
                        op=ALU.is_equal),
                    deps=[("sB_v", cnt["sB_v"])], inc="sB_v")
                bst("vector", lambda E, e8=e8: E.tensor_scalar(
                        out=sb_M[:, e8 * 16:(e8 + 1) * 16], in0=sb_onesB[:, :],
                        scalar1=sb_te[:, 0:1], scalar2=None, op0=ALU.mult),
                    deps=[("sB_v", cnt["sB_v"])], inc="sB_v")

            lnv0 = cnt["sB_v"]
            for i in range(8):
                base_v, base_a = cnt["sB_v"], cnt["sB_a"]
                st_ = lambda k: sb_bst[:, k:k + 1]
                M1, S2, Uc, SD, RS, MP, Qc = 0, 1, 2, 3, 4, 5, 6
                bst("sync", lambda E, i=i: E.dma_start(
                        out=sb_bx[:, :], in_=xo[i * 128:(i + 1) * 128, :]),
                    deps=[("sB_v", base_v)], inc="sB_x", inck=16)
                bst("vector", lambda E, st_=st_: E.reduce_sum(st_(M1), sb_bx[:, :], axis=AX.X),
                    deps=[("sB_x", cnt["sB_x"]), ("sB_a", base_a)], inc="sB_v")
                bst("scalar", lambda E: E.activation(out=sb_bsq[:, :], in_=sb_bx[:, :],
                                                     func=AF.Square),
                    deps=[("sB_x", cnt["sB_x"]), ("sB_v", base_v),
                          ("sB_tp", cnt["sB_tp"])], inc="sB_a")
                bst("scalar", lambda E, st_=st_: E.activation(out=st_(Qc), in_=st_(M1),
                                                              func=AF.Square),
                    deps=[("sB_v", cnt["sB_v"])], inc="sB_a")
                bst("vector", lambda E, st_=st_: E.reduce_sum(st_(S2), sb_bsq[:, :], axis=AX.X),
                    deps=[("sB_a", cnt["sB_a"])], inc="sB_v")
                bst("vector", lambda E, st_=st_: E.scalar_tensor_tensor(
                        out=st_(Uc), in0=st_(Qc), scalar=-1.0 / D, in1=st_(S2),
                        op0=ALU.mult, op1=ALU.add),
                    deps=[("sB_v", cnt["sB_v"]), ("sB_a", cnt["sB_a"])], inc="sB_v")
                bst("scalar", lambda E, st_=st_: E.activation(
                        out=st_(SD), in_=st_(Uc), func=AF.Sqrt,
                        bias=sb_cst[:, 1:2], scale=sb_cst[:, 0:1]),
                    deps=[("sB_v", cnt["sB_v"])], inc="sB_a")
                bst("vector", lambda E, st_=st_: E.reciprocal(st_(RS), st_(SD)),
                    deps=[("sB_a", cnt["sB_a"])], inc="sB_v")
                bst("vector", lambda E, st_=st_: E.scalar_tensor_tensor(
                        out=st_(MP), in0=st_(M1), scalar=1.0 / D, in1=st_(RS),
                        op0=ALU.mult, op1=ALU.mult),
                    deps=[("sB_v", cnt["sB_v"])], inc="sB_v")
                bst("vector", lambda E, st_=st_: E.tensor_scalar(
                        out=sb_bsq[:, :], in0=sb_bx[:, :], scalar1=st_(RS), scalar2=st_(MP),
                        op0=ALU.mult, op1=ALU.subtract),
                    deps=[("sB_v", cnt["sB_v"]), ("sB_tp", cnt["sB_tp"])], inc="sB_v")
                xnv = cnt["sB_v"]
                for g in range(16):
                    half = g % 2

                    def em_tp(E, g=g, half=half):
                        ins = None
                        for k in range(2):
                            c = 2 * g + k
                            ins = E.transpose(
                                pt[:, half * 1024 + k * 128: half * 1024 + (k + 1) * 128],
                                sb_bsq[:, c * 128:(c + 1) * 128], sb_ident[:, :])
                        return ins
                    bst("tensor", em_tp,
                        deps=[("sB_v", xnv), ("sB_ev", cnt["sB_ev"] - 1),
                              ("sA_ev", 16 * NT_KV)],
                        inc="sB_tp")

                    def em_ev(E, i=i, g=g, half=half):
                        ins = None
                        for k in range(2):
                            c = 2 * g + k
                            ins = E.copy(
                                out=sb_xnoT[:, c * TOWN + i * 128: c * TOWN + (i + 1) * 128],
                                in_=pt[:, half * 1024 + k * 128: half * 1024 + (k + 1) * 128])
                        return ins
                    bst("scalar", em_ev, deps=[("sB_tp", cnt["sB_tp"])], inc="sB_ev")

            for i in range(8):
                prev_lg_ev = cnt["sB_a"]

                def em_rt(E, i=i):
                    reg = pS[:, 0:NEXP]
                    ins = None
                    for d_ in range(NCH):
                        ins = E.matmul(reg,
                                       sb_xnoT[:, d_ * TOWN + i * 128:
                                               d_ * TOWN + (i + 1) * 128],
                                       sb_rwT[:, d_ * NEXP:(d_ + 1) * NEXP],
                                       start=(d_ == 0), stop=False)
                    ins = E.matmul(reg, sb_ones[0:1, 0:128], sb_rbbf[0:1, :],
                                   start=False, stop=True)
                    return ins
                bst("tensor", em_rt,
                    deps=[("sB_ev", 16 * (i + 1)), ("sB_a", prev_lg_ev),
                          ("sK_ev", NGK), ("sM_g", 48), ("sQ_e", 0)],
                    inc="sB_g")
                bst("scalar", lambda E: E.copy(out=sb_lg[:, 0:NEXP], in_=pS[:, 0:NEXP]),
                    deps=[("sB_g", cnt["sB_g"])], inc="sB_a")
                bst("vector", lambda E: E.reduce_max(sb_bst[:, 7:8], sb_lg[:, 0:NEXP],
                                                     axis=AX.X),
                    deps=[("sB_a", cnt["sB_a"])], inc="sB_v")
                bst("vector", lambda E: E.tensor_scalar_mul(sb_bst[:, 8:9], sb_bst[:, 7:8],
                                                            -1.0),
                    deps=[("sB_v", cnt["sB_v"])], inc="sB_v")
                bst("scalar", lambda E: E.activation(out=sb_pp[:, 0:NEXP],
                                                     in_=sb_lg[:, 0:NEXP], func=AF.Exp,
                                                     bias=sb_bst[:, 8:9], scale=1.0),
                    deps=[("sB_v", cnt["sB_v"])], inc="sB_a")
                bst("vector", lambda E: E.reduce_sum(sb_bst[:, 9:10], sb_pp[:, 0:NEXP],
                                                     axis=AX.X),
                    deps=[("sB_a", cnt["sB_a"])], inc="sB_v")
                bst("vector", lambda E: E.reciprocal(sb_bst[:, 10:11], sb_bst[:, 9:10]),
                    deps=[("sB_v", cnt["sB_v"])], inc="sB_v")
                bst("vector", lambda E: E.tensor_scalar(
                        out=sb_pr[:, 0:NEXP], in0=sb_pp[:, 0:NEXP],
                        scalar1=sb_bst[:, 10:11], scalar2=None, op0=ALU.mult),
                    deps=[("sB_v", cnt["sB_v"])], inc="sB_v")
                prv = cnt["sB_v"]
                for n in range(NEXP):
                    bst("vector", lambda E, n=n: E.tensor_scalar(
                            out=sb_G[n % 2][:, 0:NEXP], in0=sb_pr[:, 0:NEXP],
                            scalar1=sb_pr[:, n:n + 1], scalar2=None, op0=ALU.is_gt),
                        deps=[("sB_v", prv), ("sB_p", cnt["sB_p"] - 1)], inc="sB_v")
                    bst("vector", lambda E, n=n: E.reduce_sum(
                            sb_rank[:, n:n + 1], sb_G[n % 2][:, 0:NEXP], axis=AX.X),
                        deps=[("sB_v", cnt["sB_v"])], inc="sB_p")
                bst("vector", lambda E: E.tensor_single_scalar(
                        out=sb_msk[:, 0:NEXP], in_=sb_rank[:, 0:NEXP], scalar=7.5,
                        op=ALU.is_le),
                    deps=[("sB_p", cnt["sB_p"])], inc="sB_v")
                bst("vector", lambda E: E.tensor_tensor(
                        out=sb_pm[:, 0:NEXP], in0=sb_pr[:, 0:NEXP], in1=sb_msk[:, 0:NEXP],
                        op=ALU.mult),
                    deps=[("sB_v", cnt["sB_v"])], inc="sB_v")
                bst("tensor", lambda E: E.transpose(po[0:NEXP, 0:128], sb_pm[:, 0:NEXP],
                                                    sb_identf[:, :]),
                    deps=[("sB_v", cnt["sB_v"]), ("sB_a", prev_lg_ev),
                      ("sV_ev", NGV)], inc="sB_g")
                bst("scalar", lambda E, i=i: E.copy(
                        out=sb_pmT[:, i * 128:(i + 1) * 128], in_=po[0:NEXP, 0:128]),
                    deps=[("sB_g", cnt["sB_g"])], inc="sB_a")
            pmT_done = cnt["sB_a"]

            for hf in range(2):
                bst("tensor", lambda E, hf=hf: E.matmul(
                        pS[:, hf * 512:(hf + 1) * 512], sb_M[:, 0:128],
                        sb_pmT[:, hf * 512:(hf + 1) * 512], start=True, stop=True),
                    deps=[("sB_a", pmT_done), ("sB_v", cnt["sB_v"])], inc="sB_g")
                bst("vector", lambda E, hf=hf: E.tensor_copy(
                        out=sb_wexp[:, hf * 512:(hf + 1) * 512],
                        in_=pS[:, hf * 512:(hf + 1) * 512]),
                    deps=[("sB_g", cnt["sB_g"])], inc="sB_v")
            wexp_done = cnt["sB_v"]

            for hf in range(2):
                def em_h(E, hf=hf):
                    reg = pS[:, hf * 512:(hf + 1) * 512]
                    ins = None
                    for d_ in range(NCH):
                        ins = E.matmul(reg,
                                       sb_AaT[:, d_ * 128:(d_ + 1) * 128],
                                       sb_xnoT[:, d_ * TOWN + hf * 512:
                                               d_ * TOWN + hf * 512 + 512],
                                       start=(d_ == 0), stop=(d_ == NCH - 1))
                    return ins
                bst("tensor", em_h,
                    deps=[("sB_ev", 128), ("sV_ev", NGV), ("sM_g", 48)], inc="sB_g")
                bst("vector", lambda E, hf=hf: E.scalar_tensor_tensor(
                        out=sb_hw[:, hf * 512:(hf + 1) * 512],
                        in0=pS[:, hf * 512:(hf + 1) * 512], scalar=1.0,
                        in1=sb_wexp[:, hf * 512:(hf + 1) * 512],
                        op0=ALU.mult, op1=ALU.mult),
                    deps=[("sB_g", cnt["sB_g"]), ("sB_v", wexp_done)], inc="sB_v")
            hw_done = cnt["sB_v"]

            wqTr = wqT.rearrange("(c p) e -> p c e", p=128)
            for h in range(H):
                bst("gpsimd", lambda E, h=h: E.dma_start(
                        out=sb_wqs[h % 2].ap().rearrange("p (c e) -> p c e", e=128),
                        in_=wqTr[:, :, h * 128:(h + 1) * 128]),
                    deps=[("sQ_g", h - 1)], inc="sQ_w", inck=16)

                def em_q(E, h=h):
                    ins = None
                    for hf in range(2):
                        reg = pS[:, hf * 512:(hf + 1) * 512]
                        for d_ in range(NCH):
                            ins = E.matmul(reg,
                                           sb_wqs[h % 2][:, d_ * 128:(d_ + 1) * 128],
                                           sb_xnoT[:, d_ * TOWN + hf * 512:
                                                   d_ * TOWN + hf * 512 + 512],
                                           start=(d_ == 0), stop=(d_ == NCH - 1))
                    return ins
                bst("tensor", em_q,
                    deps=[("sQ_w", 16 * (h + 1)), ("sQ_e", h), ("sB_v", wexp_done),
                          ("sB_ev", 128)],
                    inc="sQ_g")
                bst("scalar", lambda E, h=h: E.copy(
                        out=sb_q[:, h * TOWN:(h + 1) * TOWN], in_=pS[:, 0:1024]),
                    deps=[("sQ_g", h + 1)], inc="sQ_e")

            bst("sync", lambda E: E.dma_start(out=xnoT_d[:, :], in_=sb_xnoT[:, :]),
                deps=[("sB_ev", 128)], inc="sB_x", inck=16)
            spill_done = cnt["sB_x"]

            if debug:
                bst("sync", lambda E: E.dma_start(out=dbg["dbg_xnoT"], in_=sb_xnoT[:, :]),
                    deps=[("sB_ev", 128)], inc="sB_x", inck=16)
                bst("sync", lambda E: E.dma_start(out=dbg["dbg_q"], in_=sb_q[:, :]),
                    deps=[("sQ_e", H)], inc="sB_x", inck=16)
                bst("sync", lambda E: E.dma_start(out=dbg["dbg_hw"], in_=sb_hw[:, :]),
                    deps=[("sB_v", hw_done)], inc="sB_x", inck=16)
                bst("sync", lambda E: E.dma_start(out=dbg["dbg_wexp"], in_=sb_wexp[:, :]),
                    deps=[("sB_v", hw_done)], inc="sB_x", inck=16)
            flush(block, pB)
            for g in reversed(b_guards):
                g.__exit__(None, None, None)

        if DO_C:
            sb_attT = sb_q
            c_guards = []

            def calloc(name, shp, dt):
                g = nc.sbuf_tensor(name, shp, dt)
                t = g.__enter__()
                c_guards.append(g)
                return t

            sb_k0 = calloc("sb_k0", [128, 4096], BF16)
            sb_k1 = calloc("sb_k1", [128, 4096], BF16)
            sb_v0 = calloc("sb_v0", [128, 32 * 129], BF16)
            sb_v1 = calloc("sb_v1", [128, 32 * 129], BF16)
            sb_p0 = calloc("sb_p0", [128, 1024], BF16)
            sb_p1 = calloc("sb_p1", [128, 1024], BF16)
            sb_as0 = calloc("sb_as0", [128, 128], BF16)
            sb_as1 = calloc("sb_as1", [128, 128], BF16)
            sb_rl = calloc("sb_rl", [128, 8], F32)
            sb_ks = [sb_k0, sb_k1]
            sb_vs = [sb_v0, sb_v1]
            sb_ps = [sb_p0, sb_p1]
            sb_ass = [sb_as0, sb_as1]

            FENCE_B = [("sB_v", cnt["sB_v"]), ("sB_a", cnt["sB_a"]),
                       ("sB_p", cnt["sB_p"]), ("sB_g", cnt["sB_g"]),
                       ("sB_x", cnt["sB_x"]), ("sB_ev", cnt["sB_ev"]),
                       ("sB_tp", cnt["sB_tp"]), ("sQ_w", 16 * H),
                       ("sQ_g", H), ("sQ_e", H), ("sM_c", cnt["sM_c"]),
                       ("sM_g", cnt["sM_g"])]
            pC = Prog(sems, fence=FENCE_B)
            if pair:
                def _poll_slot(E, slot):
                    nonce = E.value_load(sb_meta[0:1, 0:1])

                    def cond():
                        E.sem_clear(sems["sPoll"])
                        E.dma_start(out=sb_fl[:, :],
                                    in_=shfl[slot, 0:1]).then_inc(sems["sPoll"], 16)
                        E.wait_ge(sems["sPoll"], 16)
                        vv = E.value_load(sb_fl[0:1, 0:1])
                        return vv != nonce
                    with E.While(cond):
                        E.nop()

                def em_poll(E):
                    pr = get_preg(E, 1)
                    with E.If_eq(pr, 0):
                        _poll_slot(E, 0)
                    with E.Else():
                        _poll_slot(E, 1)
                pC.st("sync", em_poll,
                      deps=[("sPar", 1), ("sFl", 16), ("sK_st", 16 * NGK),
                            ("sV_st", 16 * NGV)])
            pC.st("vector", lambda E: E.memset(sb_v0[:, :], 1.0), inc="sB_v")
            cnt["sB_v"] += 1
            pC.st("vector", lambda E: E.memset(sb_v1[:, :], 1.0), inc="sB_v")
            cnt["sB_v"] += 1
            memset_v = cnt["sB_v"]

            kvVr = kvV.rearrange("(a j t) e -> a t j e", t=128, j=32)

            def emit_pv(pC_, ks):
                u = ks // 32
                j = ks % 32
                h = ks // 64

                def em_pv(E, u=u, j=j, ks=ks):
                    ins = None
                    for qb in range(8):
                        ins = E.matmul(
                            po[:, PO_OFF[qb]:PO_OFF[qb] + 129],
                            sb_ps[ks % 2][:, qb * 128:(qb + 1) * 128],
                            sb_vs[u % 2][:, j * 129:(j + 1) * 129],
                            start=(ks % 64 == 0 and qb in PO_START),
                            stop=(ks % 64 == 63),
                            skip_group_check=True)
                    return ins
                pC_.st("tensor", em_pv,
                       deps=[("sC_e", 2 * ks + 2), ("sC_am", 8 * h),
                             ("sB_a", pmT_done)],
                       inc="sC_pv")
            B_FENCE_PE = [("sQ_e", H), ("sB_v", hw_done), ("sB_g", cnt["sB_g"]),
                          ("sB_a", cnt["sB_a"])]
            for h in range(H):
                for kk in range(2):
                    u = 2 * h + kk
                    pC.st("sync", lambda E, h=h, kk=kk, u=u: E.dma_start(
                            out=sb_ks[u % 2][:, :],
                            in_=kvK[h * 128:(h + 1) * 128, kk * 4096:(kk + 1) * 4096]),
                          deps=[("sC_pv", 32 * (u - 1)), ("sK_st", 16 * NGK),
                                ("sV_st", 16 * NGV)],
                          inc="sC_kv", inck=16)
                    pC.st("sync", lambda E, h=h, kk=kk, u=u: E.dma_start(
                            out=sb_vs[u % 2].ap().rearrange("p (j c) -> p j c", c=129)
                                [:, :, 0:128],
                            in_=kvVr[kk, :, :, h * 128:(h + 1) * 128]),
                          deps=[("sB_v", memset_v)],
                          inc="sC_kv", inck=16)
                    for j in range(32):
                        ks = u * 32 + j
                        for hf in range(2):
                            def em_sth(E, h=h, u=u, j=j, hf=hf):
                                return E.matmul(
                                    pS[:, hf * 512:(hf + 1) * 512],
                                    sb_ks[u % 2][:, j * 128:(j + 1) * 128],
                                    sb_q[:, h * TOWN + hf * 512:
                                         h * TOWN + hf * 512 + 512],
                                    start=True, stop=True)
                            pC.st("tensor", em_sth,
                                  deps=[("sC_kv", 32 * (u + 1)),
                                        ("sC_e", 2 * (ks - 1) + hf + 1),
                                        ("sQ_e", H)] + B_FENCE_PE,
                                  inc="sC_s")

                            def em_exph(E, ks=ks, hf=hf):
                                return E.activation(
                                    out=sb_ps[ks % 2][:, hf * 512:(hf + 1) * 512],
                                    in_=pS[:, hf * 512:(hf + 1) * 512],
                                    func=AF.Exp, bias=sb_cst[:, 3:4],
                                    scale=sb_cst[:, 2:3])
                            pC.st("scalar", em_exph,
                                  deps=[("sC_s", 2 * ks + hf + 1),
                                        ("sC_pv", ks - 1), ("sB_v", CSTV)],
                                  inc="sC_e")
                        if ks % 64 > 0:
                            emit_pv(pC, ks - 1)
                emit_pv(pC, 64 * h + 63)

                def em_rl(E, h=h):
                    ins = None
                    for qb in range(8):
                        ins = E.reciprocal(sb_rl[:, qb:qb + 1],
                                           po[:, PO_OFF[qb] + 128:PO_OFF[qb] + 129])
                    return ins
                pC.st("vector", em_rl, deps=[("sC_pv", 64 * (h + 1))], inc="sC_rl")
                for qb in range(8):
                    pC.st("vector", lambda E, h=h, qb=qb: E.tensor_scalar(
                            out=sb_ass[qb % 2][:, :],
                            in0=po[:, PO_OFF[qb]:PO_OFF[qb] + 128],
                            scalar1=sb_rl[:, qb:qb + 1], scalar2=None, op0=ALU.mult),
                          deps=[("sC_rl", h + 1), ("sC_tr", 8 * h + qb - 1)],
                          inc="sC_am")
                    pC.st("tensor", lambda E, h=h, qb=qb: E.transpose(
                            pt[:, (qb % 2) * 1024:(qb % 2) * 1024 + 128],
                            sb_ass[qb % 2][:, :], sb_ident[:, :]),
                          deps=[("sC_am", 8 * h + qb + 1), ("sC_ae", 8 * h + qb - 1)],
                          inc="sC_tr")
                    pC.st("scalar", lambda E, h=h, qb=qb: E.copy(
                            out=sb_attT[:, h * TOWN + qb * 128: h * TOWN + (qb + 1) * 128],
                            in_=pt[:, (qb % 2) * 1024:(qb % 2) * 1024 + 128]),
                          deps=[("sC_tr", 8 * h + qb + 1), ("sC_s", 128 * (h + 1))],
                          inc="sC_ae")
            if debug:
                pC.st("sync", lambda E: E.dma_start(out=dbg["dbg_attT"], in_=sb_attT[:, :]),
                      deps=[("sC_ae", 8 * H)], inc="sC_kv", inck=16)
                pC.st("sync", lambda E: E.dma_start(
                        out=dbg["dbg_k"][0:256, :], in_=kvK[0:256, :]),
                      deps=[("sK_st", 16 * NGK)], inc="sC_kv", inck=16)
                pC.st("sync", lambda E: E.dma_start(
                        out=dbg["dbg_k"][256:384, :], in_=kvK[17 * 128:18 * 128, :]),
                      deps=[("sK_st", 16 * NGK)], inc="sC_kv", inck=16)
                pC.st("sync", lambda E: E.dma_start(
                        out=dbg["dbg_v"][:, :], in_=kvV[0:512, :]),
                      deps=[("sV_st", 16 * NGV)], inc="sC_kv", inck=16)
            flush(block, pC)
            for g in reversed(c_guards):
                g.__exit__(None, None, None)

        if DO_D:
            with (
                nc.sbuf_tensor("sb_xnoT2", [128, NCH * TOWN], BF16) as sb_xnoT2,
                nc.sbuf_tensor("sb_w10", [128, NCH * OB], BF16) as w10,
                nc.sbuf_tensor("sb_w11", [128, NCH * OB], BF16) as w11,
                nc.sbuf_tensor("sb_w20", [128, NCH * OB], BF16) as w20,
                nc.sbuf_tensor("sb_w21", [128, NCH * OB], BF16) as w21,
                nc.sbuf_tensor("sb_BaT", [128, D], BF16) as sb_BaT,
                nc.sbuf_tensor("sb_bcol", [128, NOB], F32) as sb_bcol,
                nc.sbuf_tensor("sb_xot0", [128, TOWN], F32) as xot0,
                nc.sbuf_tensor("sb_xot1", [128, TOWN], F32) as xot1,
                nc.sbuf_tensor("sb_of0", [128, TOWN], F32) as of0,
                nc.sbuf_tensor("sb_of1", [128, TOWN], F32) as of1,
            ):
                w1s, w2s = [w10, w11], [w20, w21]
                xots, ofs = [xot0, xot1], [of0, of1]
                FENCE_C = [("sC_pv", 64 * H), ("sC_e", 128 * H), ("sC_s", 128 * H),
                           ("sC_ae", 8 * H), ("sC_am", 8 * H), ("sC_tr", 8 * H),
                           ("sC_rl", H),
                           ("sC_kv", 32 * 2 * H + (64 if debug else 0))]
                pD = Prog(sems, fence=FENCE_C)
                pD.st("gpsimd", lambda E: E.dma_start(out=sb_BaT[:, :], in_=BaTd[:, :]),
                      inc="sD_w", inck=16)
                pD.st("gpsimd", lambda E: E.dma_start(
                        out=sb_bcol[:, :], in_=bbcold[:, :]),
                      inc="sD_w", inck=16)
                pD.st("sync", lambda E: E.dma_start(out=sb_xnoT2[:, :], in_=xnoT_d[:, :]),
                      deps=[("sB_x", spill_done)], inc="sM_c", inck=16)
                cnt["sM_c"] += 16
                xno2_done = cnt["sM_c"]
                woTr = woT.rearrange("(c p) e -> p c e", p=128)
                bWTr = bWT.rearrange("(c p) e -> p c e", p=128)
                DW0 = 32
                for ob in range(NOB):
                    pD.st("gpsimd", lambda E, ob=ob: E.dma_start(
                            out=w1s[ob % 2].ap().rearrange("p (c e) -> p c e", e=OB),
                            in_=woTr[:, :, ob * OB:(ob + 1) * OB]),
                          deps=[("sD_m", 2 * (ob - 1))], inc="sD_w", inck=16)
                    pD.st("gpsimd", lambda E, ob=ob: E.dma_start(
                            out=w2s[ob % 2].ap().rearrange("p (c e) -> p c e", e=OB),
                            in_=bWTr[:, :, ob * OB:(ob + 1) * OB]),
                          deps=[("sD_m", 2 * (ob - 1))], inc="sD_w", inck=16)
                    pD.st("sync", lambda E, ob=ob: E.dma_start(
                            out=xots[ob % 2][:, :],
                            in_=xoTd[ob * 128:(ob + 1) * 128, :]),
                          deps=[("sD_a", 2 * ob - 2)], inc="sD_x", inck=16)
                    for hf in range(2):
                        di = 2 * ob + hf
                        regf = (pS if ob % 2 == 0 else po)

                        def em_blk(E, ob=ob, hf=hf, regf=regf):
                            reg = regf[:, hf * 512:(hf + 1) * 512]
                            ins = None
                            for e_ in range(NCH):
                                ins = E.matmul(
                                    reg,
                                    w1s[ob % 2][:, e_ * OB:(e_ + 1) * OB],
                                    sb_attT[:, e_ * TOWN + hf * 512:
                                            e_ * TOWN + hf * 512 + 512],
                                    start=(e_ == 0), stop=False)
                            for d_ in range(NCH):
                                ins = E.matmul(
                                    reg,
                                    w2s[ob % 2][:, d_ * OB:(d_ + 1) * OB],
                                    sb_xnoT2[:, d_ * TOWN + hf * 512:
                                             d_ * TOWN + hf * 512 + 512],
                                    start=False, stop=False)
                            ins = E.matmul(reg, sb_BaT[:, ob * OB:(ob + 1) * OB],
                                           sb_hw[:, hf * 512:hf * 512 + 512],
                                           start=False, stop=True)
                            return ins
                        pD.st("tensor", em_blk,
                              deps=[("sD_w", DW0 + 32 * (ob + 1)),
                                    ("sD_a", 2 * ob + hf - 3),
                                    ("sC_ae", 8 * H), ("sC_e", 128 * H),
                                    ("sC_am", 8 * H),
                                    ("sB_v", hw_done), ("sM_c", xno2_done)],
                              inc="sD_m")
                        pD.st("vector", lambda E, ob=ob, hf=hf, regf=regf: E.tensor_scalar(
                                out=ofs[ob % 2][:, hf * 512:(hf + 1) * 512],
                                in0=regf[:, hf * 512:(hf + 1) * 512],
                                scalar1=sb_bcol[:, ob:ob + 1], scalar2=None,
                                op0=ALU.add),
                              deps=[("sD_m", di + 1), ("sD_w", DW0),
                                    ("sD_o", 16 * (di - 3)), ("sC_am", 8 * H)],
                              inc="sD_b")
                        pD.st("vector", lambda E, ob=ob, hf=hf: E.tensor_tensor(
                                out=ofs[ob % 2][:, hf * 512:(hf + 1) * 512],
                                in0=ofs[ob % 2][:, hf * 512:(hf + 1) * 512],
                                in1=xots[ob % 2][:, hf * 512:(hf + 1) * 512],
                                op=ALU.add),
                              deps=[("sD_b", di + 1), ("sD_x", 16 * (ob + 1))],
                              inc="sD_a")
                        pD.st("sync", lambda E, ob=ob, hf=hf: E.dma_start(
                                out=out[ob * 128:(ob + 1) * 128,
                                        hf * 512:(hf + 1) * 512],
                                in_=ofs[ob % 2][:, hf * 512:(hf + 1) * 512]),
                              deps=[("sD_a", di + 1)], inc="sD_o", inck=16)
                pD.st("sync", None, deps=[("sD_o", 16 * 2 * NOB)])
                flush(block, pD)
    return nc


def _prep(inputs):
    g = lambda k: np.asarray(inputs[k])
    x = np.ascontiguousarray(g("L0_attn_x").reshape(S, D).astype(np.float32, copy=False))
    m = {
        "xf": x,
        "wqT": np.ascontiguousarray(g("L0_attn_w_q").T.astype(np.float32, copy=False)),
        "wkT": np.ascontiguousarray(g("L0_attn_w_k").T.astype(np.float32, copy=False)),
        "wvT": np.ascontiguousarray(g("L0_attn_w_v").T.astype(np.float32, copy=False)),
        "woT": np.ascontiguousarray(g("L0_attn_w_o").T.astype(np.float32, copy=False)),
        "bWT": np.ascontiguousarray(g("L0_ffn_base_W").T.astype(np.float32, copy=False)),
        "rWT": np.ascontiguousarray(g("L0_ffn_router_W").T.astype(np.float32, copy=False)),
        "AaT": np.ascontiguousarray(
            g("L0_ffn_A").transpose(2, 0, 1).reshape(D, 128).astype(np.float32, copy=False)),
        "BaT": np.ascontiguousarray(
            g("L0_ffn_B").transpose(0, 2, 1).reshape(128, D).astype(np.float32, copy=False)),
        "bbcol": np.ascontiguousarray(
            g("L0_ffn_base_b").astype(np.float32, copy=False).reshape(NOB, 128).T),
        "rb": np.ascontiguousarray(g("L0_ffn_router_b").astype(np.float32, copy=False)),
        "em": np.ascontiguousarray(g("L0_ffn_expert_map").astype(np.int32, copy=False)),
        "identbf": np.eye(128, dtype=BFD),
    }
    return x, m


def kernel(**inputs):
    x, shared = _prep(inputs)
    pair = True
    nc = build_nc(pair=pair)
    nonce = int(np.random.default_rng().integers(1, 2 ** 30))
    in_maps = []
    for c in range(NCORES):
        m = dict(shared)
        m["xo"] = np.ascontiguousarray(x[c * TOWN:(c + 1) * TOWN])
        m["xoT"] = np.ascontiguousarray(x[c * TOWN:(c + 1) * TOWN].T)
        if pair:
            m["meta"] = np.array([nonce, 0, 0, 0], dtype=np.int32)
        in_maps.append(m)
    res = run_bass_kernel_spmd(nc, in_maps, core_ids=list(range(NCORES)))
    full = np.concatenate([res.results[c]["out"].T for c in range(NCORES)], axis=0)
    return np.ascontiguousarray(full).reshape(1, S, D)
